# version 15 (final)
# baseline (speedup 1.0000x reference)
from collections import defaultdict
from contextlib import ExitStack

import numpy as np
import concourse.bass as bass
import concourse.mybir as mybir
from concourse.bass_utils import run_bass_kernel_spmd

F32 = mybir.dt.float32
BF16 = mybir.dt.bfloat16
U8 = mybir.dt.uint8
AF = mybir.ActivationFunctionType
ALU = mybir.AluOpType

D = 1024
EXP = 1536
HID = 4096
NREAL = 4096
WARM = 512
T = NREAL + WARM
WA = 288
MASK_U8 = True
A2_STOP = 99
WM = 256
EPS = 1e-5

ENGINES = ("tensor", "vector", "scalar", "gpsimd", "sync")



class Buf:
    __slots__ = ("name", "ap", "writer", "readers", "lanes", "is_psum")

    def __init__(self, ap, name=""):
        self.ap = ap
        self.name = name
        self.writer = None
        self.readers = {}
        self.lanes = {}
        self.is_psum = False


class _Op:
    __slots__ = ("kind", "payload", "need_inc", "value", "lane")

    def __init__(self, kind, payload, lane=None):
        self.kind = kind
        self.payload = payload
        self.need_inc = False
        self.value = 0
        self.lane = lane


class Prog:
    def __init__(self, nc):
        self.nc = nc
        self.ops = {e: [] for e in ENGINES}
        self.sems = {}
        self._sem_ctxs = []
        self._lane_n = 0
        self.lane_val = {}
        for e in ENGINES:
            self._make_sem(e)

    def _make_sem(self, name):
        ctx = self.nc.semaphore(f"S_{name}")
        self.sems[name] = ctx.__enter__()
        self._sem_ctxs.append(ctx)
        return name

    def new_lane(self, tag=""):
        self._lane_n += 1
        return self._make_sem(f"L{self._lane_n}_{tag}")

    def _edge(self, consumer, ref):
        if ref is None:
            return
        if ref[0] == "op":
            _, pe, pidx = ref
            self.ops[pe][pidx].need_inc = True
        self.ops[consumer].append(_Op("wait", ref))

    def _last_op_idx(self, e):
        for i in range(len(self.ops[e]) - 1, -1, -1):
            if self.ops[e][i].kind in ("op", "dma"):
                return i
        return None

    def watermark(self, e):
        i = self._last_op_idx(e)
        if i is None:
            return None
        op = self.ops[e][i]
        if op.kind == "dma":
            return ("lane",) + op.lane
        return ("op", e, i)

    def _pre_edges(self, e, reads, writes):
        for b in reads:
            self._edge(e, b.writer)
            if b.is_psum:
                for key, v in list(b.readers.items()):
                    if key != e and not isinstance(key, tuple):
                        self._edge(e, ("op", key, v))
        for b in writes:
            self._edge(e, b.writer)
            for r in self._reader_refs(b):
                self._edge(e, r)

    @staticmethod
    def _add_reader(b, ref):
        if ref[0] == "op":
            _, pe, pidx = ref
            b.readers[pe] = max(b.readers.get(pe, -1), pidx)
        else:
            _, lane, v = ref
            key = ("lane", lane)
            b.readers[key] = max(b.readers.get(key, 0), v)

    def _reader_refs(self, b):
        out = []
        for key, v in b.readers.items():
            if isinstance(key, tuple):
                out.append(("lane", key[1], v))
            else:
                out.append(("op", key, v))
        return out

    def _post_update(self, e, ref, reads, writes):
        for b in reads:
            self._add_reader(b, ref)
        for b in writes:
            b.readers = {}
            b.writer = ref

    def emit(self, e, fn, reads=(), writes=(), accum=None):
        if accum is not None:
            for r in self._reader_refs(accum):
                self._edge(e, r)
            accum.readers = {}
            self._pre_edges(e, reads, ())
        else:
            self._pre_edges(e, reads, writes)
        self.ops[e].append(_Op("op", fn))
        ref = ("op", e, len(self.ops[e]) - 1)
        if accum is not None:
            accum.writer = ref
            self._post_update(e, ref, reads, ())
        else:
            self._post_update(e, ref, reads, writes)
        return ref

    def emit_dma(self, q, fn, reads=(), writes=(), group=False, lane_buf=None):
        assert len(writes) == 1
        b = writes[0]
        lb = lane_buf if lane_buf is not None else b
        role = "sw" if q == "gpsimd" else "hw"
        if role not in lb.lanes:
            lb.lanes[role] = self.new_lane(lb.name[:10])
            self.lane_val[lb.lanes[role]] = 0
        lane = lb.lanes[role]
        for rb in reads:
            self._edge(q, rb.writer)
        if group:
            pass
        else:
            self._edge(q, b.writer)
            for r in self._reader_refs(b):
                self._edge(q, r)
        self.lane_val[lane] += 16
        v = self.lane_val[lane]
        op = _Op("dma", fn, lane=(lane, v))
        self.ops[q].append(op)
        ref = ("lane", lane, v)
        for rb in reads:
            self._add_reader(rb, ref)
        b.readers = {}
        b.writer = ref
        return ref

    def claim(self, e, b):
        self._edge(e, b.writer)
        for r in self._reader_refs(b):
            self._edge(e, r)
        b.writer = None
        b.readers = {}

    def barrier(self, engines=ENGINES):
        marks = {e: self.watermark(e) for e in engines}
        for e in engines:
            for pe, ref in marks.items():
                if pe != e:
                    self._edge(e, ref)
            for lane, v in self.lane_val.items():
                if v > 0:
                    self._edge(e, ("lane", lane, v))

    def wait_ref(self, e, ref):
        self._edge(e, ref)

    def finalize(self, block):
        for e in ENGINES:
            c = 0
            for op in self.ops[e]:
                if op.kind == "op" and op.need_inc:
                    c += 1
                    op.value = c

        for e in ENGINES:
            stream = self.ops[e]
            if not stream:
                continue

            def make_body(stream=stream, e=e):
                def body(eng):
                    watermark = defaultdict(int)
                    for op in stream:
                        if op.kind == "wait":
                            ref = op.payload
                            if ref[0] == "op":
                                _, pe, pidx = ref
                                v = self.ops[pe][pidx].value
                                assert v > 0, f"wait on {pe}[{pidx}] without inc"
                                sem = pe
                            else:
                                _, sem, v = ref
                            if watermark[sem] >= v:
                                continue
                            eng.wait_ge(self.sems[sem], v)
                            watermark[sem] = v
                        elif op.kind == "op":
                            instr = op.payload(eng)
                            if op.need_inc:
                                instr.then_inc(self.sems[e], 1)
                        elif op.kind == "dma":
                            op.payload(eng).then_inc(self.sems[op.lane[0]], 16)
                return body

            getattr(block, e)(make_body())

    def stats(self):
        out = {}
        for e in ENGINES:
            out[e] = dict(
                ops=sum(1 for o in self.ops[e] if o.kind == "op"),
                dmas=sum(1 for o in self.ops[e] if o.kind == "dma"),
                waits=sum(1 for o in self.ops[e] if o.kind == "wait"),
                incs=sum(1 for o in self.ops[e] if o.kind == "op" and o.need_inc),
            )
        out["lanes"] = self._lane_n
        return out



PARAM_LAYOUT = [
    ("pw_b", 8), ("dw_b", 8),
    ("ln1_g", 8), ("ln1_b", 8),
    ("ln2_g", 8), ("ln2_b", 8),
    ("ln3_g", 8), ("ln3_b", 8),
    ("mlp_b2", 8), ("fb2", 8),
    ("fln1_g", 8), ("fln1_b", 8),
    ("f_b", 24),
    ("mlp_b1", 32), ("fb1", 32),
    ("fln2_g", 32), ("fln2_b", 32),
    ("dwtaps", 32),
]

PROW_LAYOUT = [
    ("f_b", 2 * EXP), ("pw_b", D), ("mlp_b1", HID), ("mlp_b2", D),
    ("fb1", HID), ("fb2", D), ("dw_b", D), ("ln1_b", D), ("dwtaps", 4 * D),
]
PROW_OFF = {}
_poff = 0
for _nm, _nc2 in PROW_LAYOUT:
    PROW_OFF[_nm] = _poff
    _poff += _nc2
PROW_COLS = _poff
PARAM_OFF = {}
_off = 0
for _nm, _nc_ in PARAM_LAYOUT:
    PARAM_OFF[_nm] = _off
    _off += _nc_
PARAM_COLS = _off


def _cols(v):
    return np.ascontiguousarray(np.asarray(v, np.float32).reshape(-1, 128).T)


def _tiles(w_T):
    w_T = np.asarray(w_T, np.float32)
    iin, out = w_T.shape
    return np.ascontiguousarray(w_T.reshape(iin // 128, 128, out).transpose(1, 0, 2))



def build_program(phases=("A1", "A2", "B", "C1", "C2"), debug_out=None):
    nc = bass.Bass("TRN2", target_bir_lowering=False)

    xd = nc.dram_tensor("x", [128, 8, T], F32, kind="ExternalInput").ap()
    smaskd = nc.dram_tensor("smask", [128, 2], F32, kind="ExternalInput").ap()
    pcolsd = nc.dram_tensor("pcols", [128, PARAM_COLS], F32, kind="ExternalInput").ap()
    prowsd = nc.dram_tensor("prows", [1, PROW_COLS], F32, kind="ExternalInput").ap()
    identd = nc.dram_tensor("ident", [128, 128], F32, kind="ExternalInput").ap()
    onesrowd = nc.dram_tensor("onesrow", [1, 128], F32, kind="ExternalInput").ap()
    wpwd = nc.dram_tensor("wpw", [128, 8, D], F32, kind="ExternalInput").ap()
    wfwd = nc.dram_tensor("wfw", [128, 8, 2 * EXP], F32, kind="ExternalInput").ap()
    wdpd = nc.dram_tensor("wdp", [128, 12, D], F32, kind="ExternalInput").ap()
    wm1d = nc.dram_tensor("wm1", [128, 8, HID], F32, kind="ExternalInput").ap()
    wm2d = nc.dram_tensor("wm2", [128, 32, D], F32, kind="ExternalInput").ap()
    ww1d = nc.dram_tensor("ww1", [128, 8, HID], F32, kind="ExternalInput").ap()
    ww2d = nc.dram_tensor("ww2", [128, 32, D], F32, kind="ExternalInput").ap()
    yd = nc.dram_tensor("y", [128, 8, NREAL], F32, kind="ExternalOutput").ap()

    def scr(name, shape, dt=F32):
        kind = "ExternalOutput" if debug_out == name else "Internal"
        return nc.dram_tensor(name, shape, dt, kind=kind).ap()

    xb_dram = scr("xb_scr", [128, 8, T])
    xc_dram = scr("xc_scr", [128, 8, T])
    xm_dram = scr("xm_scr", [128, 8, NREAL])
    hn_dram = scr("hn_scr", [128, 32, NREAL], BF16)

    x_in = Buf(xd, "x_in")
    xb_d = Buf(xb_dram, "xb_d")
    xc_d = Buf(xc_dram, "xc_d")
    xm_d = Buf(xm_dram, "xm_d")
    hn_d = Buf(hn_dram, "hn_d")
    y_out = Buf(yd, "y_out")

    p = Prog(nc)
    stack = ExitStack()

    def sb(name, shape, dt=F32, st=None):
        h = (st or stack).enter_context(nc.sbuf_tensor(name, shape, dt))
        return Buf(h.ap(), name)

    def ps(name, banks=1, st=None):
        h = (st or stack).enter_context(nc.psum_tensor(name, [128, 512 * banks], F32))
        b = Buf(h.ap(), name)
        b.is_psum = True
        return b

    def r3(ap, k=8):
        return ap.rearrange("p (k w) -> p k w", k=k)

    pc = sb("pcols_sb", [128, PARAM_COLS])
    onesrow = sb("onesrow_sb", [1, 128])
    onesw_bf = sb("onesw_bf", [1, 512], BF16)
    smask = sb("smask_sb", [128, 2])
    ones_bf = sb("ones_bf", [128, 1], BF16)
    carry = sb("carry", [128, 12])
    epsrow = sb("epsrow", [1, 1])
    rowarena = sb("rowarena", [1, HID + D], BF16)

    def col(nm, i=0, n=1):
        o = PARAM_OFF[nm] + i
        return pc.ap[:, o:o + n]

    def load_row(name, tag, st, bf=False):
        o = PROW_OFF[name]
        n = dict(PROW_LAYOUT)[name]
        rb = sb(f"prw_{tag}", [1, n], F32, st=st)
        p.emit_dma("sync", lambda q, rb=rb, o=o, n=n: q.dma_start(
            out=rb.ap, in_=prowsd[0:1, o:o + n]), writes=[rb])
        if not bf:
            return rb
        rbb = sb(f"prwb_{tag}", [1, n], BF16, st=st)
        p.emit("vector", lambda e, rb=rb, rbb=rbb: e.tensor_copy(rbb.ap, rb.ap),
               reads=[rb], writes=[rbb])
        return rb, rbb

    def mk_rows(tag):
        rows = {}
        rows["negm"] = sb(f"row_negm_{tag}", [1, WA])
        rows["w2"] = sb(f"row_w2_{tag}", [1, 2 * WA])
        rows["sdp"] = sb(f"row_sd_{tag}", [1, 2 * WA])
        rows["invp"] = sb(f"row_inv_{tag}", [1, 2 * WA])
        rows["var"] = rows["sdp"]
        return rows

    rows2 = [mk_rows("a"), mk_rows("b")]
    rq = {}
    for nm in ["tot", "m1", "thr"]:
        rq[nm] = sb("row_" + nm, [1, 64])


    def ln_stats(rows, xbuf, width, nk, xbf, sqbf, pstat, composed=False,
                 pstat2=None, off2=None):
        if pstat2 is None:
            pstat2, off2, fresh2 = pstat, width, False
        else:
            fresh2 = True
        p.emit("vector", lambda e, xbuf=xbuf, xbf=xbf, n=nk * width: e.tensor_copy(
            xbf.ap[:, 0:n], xbuf.ap[:, 0:n]), reads=[xbuf], writes=[xbf])
        p.emit("scalar", lambda e, xbuf=xbuf, sqbf=sqbf, n=nk * width: e.activation(
            sqbf.ap[:, 0:n], xbuf.ap[:, 0:n], AF.Square), reads=[xbuf], writes=[sqbf])
        for k in range(nk):
            p.emit("tensor", lambda e, k=k, xbf=xbf, pstat=pstat, width=width, nk=nk: e.matmul(
                pstat.ap[0:1, 0:width], ones_bf.ap, xbf.ap[:, k * width:(k + 1) * width],
                start=(k == 0), stop=(k == nk - 1), skip_group_check=True),
                reads=[ones_bf, xbf], writes=[pstat] if k == 0 else (),
                accum=None if k == 0 else pstat)
        for k in range(nk):
            p.emit("tensor", lambda e, k=k, sqbf=sqbf, pstat2=pstat2, off2=off2,
                   width=width, nk=nk, fresh2=fresh2: e.matmul(
                pstat2.ap[0:1, off2:off2 + width], ones_bf.ap,
                sqbf.ap[:, k * width:(k + 1) * width],
                start=(k == 0 and fresh2), stop=(k == nk - 1), skip_group_check=True),
                reads=[ones_bf, sqbf], writes=() if pstat2 is pstat else ([pstat2] if k == 0 else ()),
                accum=pstat2 if (pstat2 is pstat or k > 0) else None)
        row_chain(rows, pstat, pstat2, off2, width, nk * 128, composed)

    def row_chain(rows, pstat, pstat2, off2, W, F, composed):
        p.emit("vector", lambda e, pstat=pstat, F=F, W=W, rows=rows: e.tensor_single_scalar(
            rows["negm"].ap[:, 0:W], pstat.ap[0:1, 0:W], -1.0 / F, ALU.mult),
            reads=[pstat], writes=[rows["negm"]])
        p.emit("vector", lambda e, W=W, rows=rows: e.tensor_tensor(
            rows["w2"].ap[:, 0:W], rows["negm"].ap[:, 0:W], rows["negm"].ap[:, 0:W],
            ALU.mult),
            reads=[rows["negm"]], writes=[rows["w2"]])
        p.emit("vector", lambda e, pstat2=pstat2, off2=off2, F=F, W=W, rows=rows:
               e.scalar_tensor_tensor(
            rows["var"].ap[:, 0:W], pstat2.ap[0:1, off2:off2 + W], 1.0 / F,
            rows["w2"].ap[:, 0:W], ALU.mult, ALU.subtract),
            reads=[pstat2, rows["w2"]], writes=[rows["var"]])
        p.emit("vector", lambda e, W=W, rows=rows: e.tensor_single_scalar(
            rows["var"].ap[:, 0:W], rows["var"].ap[:, 0:W], 0.0, ALU.max),
            reads=[rows["var"]], writes=[rows["var"]])
        p.emit("vector", lambda e, W=W, rows=rows: e.tensor_single_scalar(
            rows["w2"].ap[:, 0:W], rows["var"].ap[:, 0:W], float(EPS), ALU.add),
            reads=[rows["var"]], writes=[rows["w2"]])
        R = 2 * W if composed else W
        if composed:
            p.emit("vector", lambda e, W=W, rows=rows: e.tensor_scalar(
                rows["w2"].ap[:, W:2 * W], rows["var"].ap[:, 0:W],
                1.0 + float(EPS), float(EPS) * float(EPS), ALU.mult, ALU.add),
                reads=[rows["var"]], writes=[rows["w2"]])
        p.emit("scalar", lambda e, R=R, rows=rows: e.activation(
            rows["sdp"].ap[:, 0:R], rows["w2"].ap[:, 0:R], AF.Sqrt),
            reads=[rows["w2"]], writes=[rows["sdp"]])
        p.emit("vector", lambda e, R=R, rows=rows: e.reciprocal(
            rows["invp"].ap[:, 0:R], rows["sdp"].ap[:, 0:R]),
            reads=[rows["sdp"]], writes=[rows["invp"]])

    def bcast(rows, row, width, pbank, off=0, fresh=True):
        if row == "negm":
            buf, src = rows["negm"], rows["negm"].ap[:, 0:width]
        elif row == "inv":
            buf, src = rows["invp"], rows["invp"].ap[:, 0:width]
        else:
            buf, src = rows["invp"], rows["invp"].ap[:, width:2 * width]
        p.emit("tensor", lambda e, width=width, pbank=pbank, off=off,
               fresh=fresh, buf=buf, src=src: e.matmul(
            pbank.ap[:, off:off + width], onesrow.ap, src,
            start=fresh, stop=True, skip_group_check=True),
            reads=[onesrow, buf],
            writes=[pbank] if fresh else (), accum=None if fresh else pbank)

    def apply2(src, dst, width, nk, pneg, noff, pinv, ioff):
        p.emit("vector", lambda e, src=src, dst=dst, width=width, nk=nk, pneg=pneg,
               noff=noff: e.tensor_tensor(
            r3(dst.ap[:, 0:nk * width], nk), r3(src.ap[:, 0:nk * width], nk),
            pneg.ap[:, noff:noff + width].rearrange("p (o w) -> p o w", o=1).broadcast_to(
                (128, nk, width)), ALU.add),
            reads=[src, pneg], writes=[dst])
        p.emit("vector", lambda e, dst=dst, width=width, nk=nk, pinv=pinv,
               ioff=ioff: e.tensor_tensor(
            r3(dst.ap[:, 0:nk * width], nk), r3(dst.ap[:, 0:nk * width], nk),
            pinv.ap[:, ioff:ioff + width].rearrange("p (o w) -> p o w", o=1).broadcast_to(
                (128, nk, width)), ALU.mult),
            reads=[dst, pinv], writes=[dst])

    def pair_view(pbuf, w):
        return pbuf.ap.rearrange("p (b c) -> p b c", b=2)[:, :, 0:w]

    p.emit_dma("sync", lambda q: q.dma_start(out=pc.ap, in_=pcolsd), writes=[pc])
    p.emit_dma("sync", lambda q: q.dma_start(out=onesrow.ap, in_=onesrowd), writes=[onesrow])
    p.emit_dma("sync", lambda q: q.dma_start(out=smask.ap, in_=smaskd), writes=[smask])
    p.emit("vector", lambda e: e.memset(ones_bf.ap, 1.0), writes=[ones_bf])
    p.emit("vector", lambda e: e.memset(onesw_bf.ap, 1.0), writes=[onesw_bf])
    p.emit("vector", lambda e: e.memset(epsrow.ap, float(EPS)), writes=[epsrow])
    p.emit("vector", lambda e: e.memset(carry.ap, 0.5), writes=[carry])

    stackA = ExitStack()
    wfw = sb("wfw_bf", [128, 8 * 2 * EXP], BF16, st=stackA)
    wdp = sb("wdp_bf", [128, 12 * D], BF16, st=stackA)
    fbeff_row = sb("fbeff_row", [1, 2 * EXP], BF16, st=stackA)
    stackA1 = ExitStack()
    diag = sb("diag", [128, 32 * 128], BF16, st=stackA1)
    wpw = sb("wpw_bf", [128, 8 * D], BF16, st=stackA1)
    dwbeff_row = sb("dwbeff_row", [1, D], BF16, st=stackA1)
    pwb_row = sb("pwb_row", [1, D], BF16, st=stackA1)

    def load_weights(dst, dram_ap, nk, ncols, stages, scale_col=None, queue="scalar"):
        p.claim("vector", dst)
        for k in range(nk):
            stg = stages[k % 2]
            p.emit_dma(queue, lambda q, k=k, stg=stg, dram_ap=dram_ap, ncols=ncols:
                       q.dma_start(out=stg.ap[:, 0:ncols], in_=dram_ap[:, k, :]),
                       writes=[stg])
            if scale_col is None:
                p.emit("vector", lambda e, k=k, dst=dst, stg=stg, ncols=ncols:
                       e.tensor_copy(dst.ap[:, k * ncols:(k + 1) * ncols],
                                     stg.ap[:, 0:ncols]),
                       reads=[stg], writes=[dst] if k == nk - 1 else ())
            else:
                p.emit("vector", lambda e, k=k, dst=dst, stg=stg, ncols=ncols,
                       sc=scale_col(k): e.tensor_scalar(
                           dst.ap[:, k * ncols:(k + 1) * ncols], stg.ap[:, 0:ncols],
                           sc, None, ALU.mult),
                       reads=[stg, pc], writes=[dst] if k == nk - 1 else ())

    def prologue_A():
        with ExitStack() as st0:
            ident = sb("ident_sb", [128, 128], st=st0)
            dwg = sb("dwg", [128, 32], st=st0)
            drow = sb("drow", [1, D], st=st0)
            stages = [sb(f"wstgA{i}", [128, 4096], F32, st=st0) for i in range(2)]

            p.emit_dma("sync", lambda q, ident=ident: q.dma_start(out=ident.ap, in_=identd),
                       writes=[ident])
            p.claim("vector", dwg)
            for s in range(8):
                p.emit("vector", lambda e, s=s, dwg=dwg: e.tensor_scalar(
                    dwg.ap[:, s * 4:(s + 1) * 4], col("dwtaps", s * 4, 4), col("ln1_g", s),
                    None, ALU.mult),
                    reads=[pc], writes=[dwg] if s == 7 else ())
            p.claim("vector", diag)
            for i in range(32):
                p.emit("vector", lambda e, i=i, ident=ident, dwg=dwg: e.tensor_scalar(
                    diag.ap[:, i * 128:(i + 1) * 128], ident.ap, dwg.ap[:, i:i + 1],
                    None, ALU.mult),
                    reads=[ident, dwg], writes=[diag] if i == 31 else ())

            st0rows = stages[0]
            st1rows = stages[1]
            p.emit_dma("sync", lambda q, st1rows=st1rows: q.dma_start(
                out=st1rows.ap[0:1, 0:4 * D],
                in_=prowsd[0:1, PROW_OFF["dwtaps"]:PROW_OFF["dwtaps"] + 4 * D]),
                writes=[st1rows])
            p.emit_dma("sync", lambda q, st0rows=st0rows: q.dma_start(
                out=st0rows.ap[0:1, 0:D],
                in_=prowsd[0:1, PROW_OFF["dw_b"]:PROW_OFF["dw_b"] + D]),
                writes=[st0rows])
            p.emit_dma("sync", lambda q, st0rows=st0rows: q.dma_start(
                out=st0rows.ap[0:1, D:2 * D],
                in_=prowsd[0:1, PROW_OFF["ln1_b"]:PROW_OFF["ln1_b"] + D]),
                writes=[st0rows], group=True)
            p.emit_dma("sync", lambda q, st0rows=st0rows: q.dma_start(
                out=st0rows.ap[0:1, 2 * D:3 * D],
                in_=prowsd[0:1, PROW_OFF["pw_b"]:PROW_OFF["pw_b"] + D]),
                writes=[st0rows], group=True)
            p.emit("vector", lambda e, st0rows=st0rows: e.tensor_copy(
                pwb_row.ap, st0rows.ap[0:1, 2 * D:3 * D]),
                reads=[st0rows], writes=[pwb_row])
            p.emit("vector", lambda e, drow=drow, st1rows=st1rows: e.tensor_reduce(
                drow.ap[:, 0:D].rearrange("o (c u) -> o c u", u=1),
                st1rows.ap[0:1, 0:4 * D].rearrange("o (c k) -> o c k", k=4),
                mybir.AxisListType.X, ALU.add), reads=[st1rows], writes=[drow])
            p.emit("vector", lambda e, drow=drow, st0rows=st0rows: e.tensor_tensor(
                drow.ap[:, 0:D], drow.ap[:, 0:D], st0rows.ap[0:1, D:2 * D], ALU.mult),
                reads=[drow, st0rows], writes=[drow])
            p.emit("vector", lambda e, drow=drow, st0rows=st0rows: e.tensor_tensor(
                drow.ap[:, 0:D], drow.ap[:, 0:D], st0rows.ap[0:1, 0:D], ALU.add),
                reads=[drow, st0rows], writes=[drow])
            p.emit("vector", lambda e, drow=drow: e.tensor_copy(
                dwbeff_row.ap, drow.ap[:, 0:D]), reads=[drow], writes=[dwbeff_row])

            load_weights(wpw, wpwd, 8, D, stages)
            load_weights(wfw, wfwd, 8, 2 * EXP, stages, scale_col=lambda k: col("ln2_g", k))
            load_weights(wdp, wdpd, 12, D, stages)
            p.barrier()

    prologue_A()

    def phase_A1():
        NT = T // WA
        with ExitStack() as stA:
            xt = [sb(f"xtA1_{i}", [128, 8 * WA], st=stA) for i in range(3)]
            tw = sb("twA1", [128, 8 * WA], st=stA)
            h1bf = [sb(f"h1bfA1_{i}", [128, 8 * (WA + 3)], BF16, st=stA) for i in range(2)]
            xbf = sb("xbfA1", [128, 8 * WA], BF16, st=stA)
            sqbf = sb("sqbfA1", [128, 8 * WA], BF16, st=stA)
            cbf = sb("cbfA1", [128, 8 * WA], BF16, st=stA)
            ps1 = ps("psA1_s1", st=stA)
            ps2 = ps("psA1_s2", st=stA)
            pb1 = ps("psA1_b1", st=stA)
            pb2 = ps("psA1_b2", st=stA)
            pmc = ps("psA1_conv", banks=2, st=stA)
            pmw = ps("psA1_pw", banks=2, st=stA)

            p.emit("vector", lambda e: e.memset(
                h1bf[1].ap.rearrange("p (s c) -> p s c", s=8)[:, :, WA:WA + 3], 0.0),
                writes=[h1bf[1]])

            def front_a(j):
                xcur = xt[j % 3]
                rows = rows2[j % 2]
                t0 = j * WA
                p.emit_dma("sync", lambda q, t0=t0, xcur=xcur: q.dma_start(
                    out=r3(xcur.ap), in_=xd[:, :, t0:t0 + WA]), writes=[xcur])
                ln_stats(rows, xcur, WA, 8, xbf, sqbf, ps1, composed=True,
                         pstat2=ps2, off2=0)

            def front_b(j):
                xcur = xt[j % 3]
                hcur = h1bf[j % 2]
                hprev = h1bf[(j + 1) % 2]
                rows = rows2[j % 2]
                bcast(rows, "negm", WA, pb1)
                bcast(rows, "inv", WA, pb2)
                p.emit("vector", lambda e, xcur=xcur: e.tensor_tensor(
                    r3(tw.ap), r3(xcur.ap),
                    pb1.ap[:, 0:WA].rearrange("p (o w) -> p o w", o=1).broadcast_to(
                        (128, 8, WA)), ALU.add), reads=[xcur, pb1], writes=[tw])
                p.emit("vector", lambda e, xcur=xcur: e.tensor_tensor(
                    r3(xcur.ap), r3(tw.ap),
                    pb2.ap[:, 0:WA].rearrange("p (o w) -> p o w", o=1).broadcast_to(
                        (128, 8, WA)), ALU.mult), reads=[tw, pb2], writes=[xcur])
                bcast(rows, "inv12", WA, pb1)
                p.emit("vector", lambda e: e.tensor_tensor(
                    r3(tw.ap), r3(tw.ap),
                    pb1.ap[:, 0:WA].rearrange("p (o w) -> p o w", o=1).broadcast_to(
                        (128, 8, WA)), ALU.mult), reads=[tw, pb1], writes=[tw])
                p.emit("scalar", lambda e, hcur=hcur: e.activation(
                    hcur.ap.rearrange("p (s c) -> p s c", s=8)[:, :, 3:WA + 3], r3(tw.ap),
                    AF.Copy), reads=[tw], writes=[hcur])
                p.emit("gpsimd", lambda e, hcur=hcur, hprev=hprev: e.tensor_copy(
                    hcur.ap.rearrange("p (s c) -> p s c", s=8)[:, :, 0:3],
                    hprev.ap.rearrange("p (s c) -> p s c", s=8)[:, :, WA:WA + 3]),
                    reads=[hprev], writes=[hcur])

            def back1(j):
                hcur = h1bf[j % 2]
                p.claim("scalar", cbf)
                for sp in range(4):
                    for half in range(2):
                        s = sp * 2 + half
                        po = half * 512
                        p.emit("tensor", lambda e, s=s, po=po: e.matmul(
                            pmc.ap[:, po:po + WA], dwbeff_row.ap[:, s * 128:(s + 1) * 128],
                            onesw_bf.ap[:, 0:WA], start=True, stop=False,
                            skip_group_check=True),
                            reads=[dwbeff_row, onesw_bf],
                            writes=[pmc] if half == 0 else (),
                            accum=None if half == 0 else pmc)
                        for k in range(4):
                            p.emit("tensor", lambda e, s=s, k=k, po=po, hcur=hcur: e.matmul(
                                pmc.ap[:, po:po + WA],
                                diag.ap[:, (s * 4 + k) * 128:(s * 4 + k + 1) * 128],
                                hcur.ap[:, s * (WA + 3) + k: s * (WA + 3) + k + WA],
                                start=False, stop=(k == 3), skip_group_check=True),
                                reads=[diag, hcur], writes=(), accum=pmc)
                    p.emit("scalar", lambda e, sp=sp: e.activation(
                        cbf.ap[:, sp * 2 * WA:(sp + 1) * 2 * WA].rearrange(
                            "p (b w) -> p b w", b=2), pair_view(pmc, WA), AF.Copy),
                        reads=[pmc], writes=[cbf] if sp == 3 else ())

            def back2(j):
                xcur = xt[j % 3]
                t0 = j * WA
                for op_ in range(4):
                    for half in range(2):
                        o = op_ * 2 + half
                        po = half * 512
                        p.emit("tensor", lambda e, o=o, po=po: e.matmul(
                            pmw.ap[:, po:po + WA], pwb_row.ap[:, o * 128:(o + 1) * 128],
                            onesw_bf.ap[:, 0:WA], start=True, stop=False,
                            skip_group_check=True),
                            reads=[pwb_row, onesw_bf],
                            writes=[pmw] if half == 0 else (),
                            accum=None if half == 0 else pmw)
                        for k in range(8):
                            p.emit("tensor", lambda e, o=o, k=k, po=po: e.matmul(
                                pmw.ap[:, po:po + WA],
                                wpw.ap[:, k * D + o * 128: k * D + o * 128 + 128],
                                cbf.ap[:, k * WA:(k + 1) * WA],
                                start=False, stop=(k == 7), skip_group_check=True),
                                reads=[wpw, cbf], writes=(), accum=pmw)
                    p.emit("vector", lambda e, op_=op_, xcur=xcur: e.tensor_tensor(
                        r3(xcur.ap[:, op_ * 2 * WA:(op_ + 1) * 2 * WA], 2),
                        pair_view(pmw, WA),
                        r3(xcur.ap[:, op_ * 2 * WA:(op_ + 1) * 2 * WA], 2), ALU.add),
                        reads=[pmw, xcur], writes=[xcur])

                p.emit_dma("gpsimd", lambda q, t0=t0, xcur=xcur: q.dma_start(
                    out=xb_dram[:, :, t0:t0 + WA], in_=r3(xcur.ap)),
                    reads=[xcur], writes=[xb_d], group=True, lane_buf=xcur)

            for j in range(NT):
                front_a(j)
                if j >= 1:
                    back1(j - 1)
                front_b(j)
                if j >= 1:
                    back2(j - 1)
            back1(NT - 1)
            back2(NT - 1)
            p.barrier()

    if "A1" in phases:
        phase_A1()
    stackA1.close()

    def phase_A2():
        NT = T // WM
        with ExitStack() as stF:
            gi = sb("gi2col", [128, 16], F32, st=stF)
            fb_r = load_row("f_b", "fbA2", stF)
            gib = sb("gib2", [128, 8], BF16, st=stF)
            frow = sb("frow", [1, 2 * EXP], F32, st=stF)
            pq = ps("psA2_f", st=stF)
            p.emit("vector", lambda e, gi=gi: e.reciprocal(
                gi.ap[:, 0:8], col("ln2_g", 0, 8)), reads=[pc], writes=[gi])
            p.emit("vector", lambda e, gi=gi: e.tensor_tensor(
                gi.ap[:, 8:16], col("ln2_b", 0, 8), gi.ap[:, 0:8], ALU.mult),
                reads=[pc, gi], writes=[gi])
            p.emit("vector", lambda e, gi=gi, gib=gib: e.tensor_copy(
                gib.ap, gi.ap[:, 8:16]), reads=[gi], writes=[gib])
            p.claim("vector", frow)
            for cc in range(6):
                for k in range(8):
                    p.emit("tensor", lambda e, cc=cc, k=k, gib=gib, pq=pq: e.matmul(
                        pq.ap[0:1, 0:512], gib.ap[:, k:k + 1],
                        wfw.ap[:, k * 2 * EXP + cc * 512: k * 2 * EXP + cc * 512 + 512],
                        start=(k == 0), stop=(k == 7)),
                        reads=[gib, wfw], writes=[pq] if k == 0 else (),
                        accum=None if k == 0 else pq)
                p.emit("vector", lambda e, cc=cc, frow=frow, pq=pq, fb_r=fb_r:
                       e.scalar_tensor_tensor(
                    frow.ap[:, cc * 512:(cc + 1) * 512], pq.ap[0:1, 0:512], 1.0,
                    fb_r.ap[:, cc * 512:(cc + 1) * 512], ALU.mult, ALU.add),
                    reads=[pq, fb_r], writes=[frow] if cc == 5 else ())
            p.emit("vector", lambda e, frow=frow: e.tensor_copy(fbeff_row.ap, frow.ap),
                   reads=[frow], writes=[fbeff_row])
            p.barrier()

        with ExitStack() as stA:
            xbt = [sb(f"xbtA2_{i}", [128, 8 * WM], st=stA) for i in range(3)]
            twa = sb("twA2", [128, 8 * WM], st=stA)
            xbf2 = sb("xbfA2", [128, 8 * WM], BF16, st=stA)
            sqbf2 = sb("sqbfA2", [128, 8 * WM], BF16, st=stA)
            gbf = [sb(f"gbfA2_{i}", [128, 8 * WM], BF16, st=stA) for i in range(2)]
            h_raw = sb("h_rawA2", [128, 2 * WM], st=stA)
            a_s = sb("a_sA2", [128, 2 * WM], st=stA)
            z_s = sb("z_sA2", [128, 2 * WM], st=stA)
            gt_s = sb("gt_sA2", [128, 2 * WM], st=stA)
            v_s = sb("v_sA2", [128, 2 * WM], st=stA)
            mk_s = sb("mk_sA2", [128, 2 * WM], U8, st=stA)
            hseq = sb("hseqA2", [128, 12 * WM], st=stA)
            hbf = sb("hbfA2", [128, 12 * WM], BF16, st=stA)
            xct = [sb("xctA2_0", [128, 8 * WM], st=stA)] * 2
            psb = ps("psA2_sb", st=stA)
            pbb = ps("psA2_bb", st=stA)
            pmKs = [ps(f"psA2_K{i}", banks=2, st=stA) for i in range(2)]
            pmH = ps("psA2_H", banks=2, st=stA)

            def front_a(j):
                xcur = xbt[j % 3]
                rows = rows2[j % 2]
                t0 = j * WM
                p.emit_dma("sync", lambda q, t0=t0, xcur=xcur: q.dma_start(
                    out=r3(xcur.ap), in_=xb_dram[:, :, t0:t0 + WM]),
                    reads=[xb_d], writes=[xcur])
                ln_stats(rows, xcur, WM, 8, xbf2, sqbf2, psb)

            def front_b(j):
                xcur = xbt[j % 3]
                rows = rows2[j % 2]
                gcur = gbf[j % 2]
                bcast(rows, "negm", WM, pbb, 0, True)
                bcast(rows, "inv", WM, pbb, WM, False)
                apply2(xcur, twa, WM, 8, pbb, 0, pbb, WM)
                p.emit("scalar", lambda e, gcur=gcur: e.activation(
                    gcur.ap, twa.ap, AF.Copy), reads=[twa], writes=[gcur])

            def back_q(j, cps):
                gcur = gbf[j % 2]
                warm_cols = min(WM, max(0, WARM - j * WM))
                for cp in cps:
                    c0 = cp * 2
                    pmK = pmKs[cp % 2]
                    for half in range(2):
                        c = c0 + half
                        po = half * 512
                        p.emit("tensor", lambda e, c=c, po=po, pmK=pmK: e.matmul(
                            pmK.ap[:, po:po + WM], fbeff_row.ap[:, c * 128:(c + 1) * 128],
                            onesw_bf.ap[:, 0:WM], start=True, stop=False, skip_group_check=True),
                            reads=[fbeff_row, onesw_bf],
                            writes=[pmK] if half == 0 else (),
                            accum=None if half == 0 else pmK)
                        for k in range(8):
                            p.emit("tensor", lambda e, c=c, k=k, po=po, pmK=pmK,
                                   gcur=gcur: e.matmul(
                                pmK.ap[:, po:po + WM],
                                wfw.ap[:, k * 2 * EXP + c * 128: k * 2 * EXP + c * 128 + 128],
                                gcur.ap[:, k * WM:(k + 1) * WM],
                                start=False, stop=(k == 7), skip_group_check=True),
                                reads=[wfw, gcur], writes=(), accum=pmK)
                    for half in range(2):
                        c = c0 + half
                        po = half * 512
                        p.emit("tensor", lambda e, c=c, po=po: e.matmul(
                            pmH.ap[:, po:po + WM],
                            fbeff_row.ap[:, (12 + c) * 128:(13 + c) * 128],
                            onesw_bf.ap[:, 0:WM], start=True, stop=False, skip_group_check=True),
                            reads=[fbeff_row, onesw_bf],
                            writes=[pmH] if half == 0 else (),
                            accum=None if half == 0 else pmH)
                        for k in range(8):
                            p.emit("tensor", lambda e, c=c, k=k, po=po, gcur=gcur: e.matmul(
                                pmH.ap[:, po:po + WM],
                                wfw.ap[:, k * 2 * EXP + (12 + c) * 128: k * 2 * EXP + (12 + c) * 128 + 128],
                                gcur.ap[:, k * WM:(k + 1) * WM],
                                start=False, stop=(k == 7), skip_group_check=True),
                                reads=[wfw, gcur], writes=(), accum=pmH)
                    p.emit("scalar", lambda e, pmK=pmK: e.activation(
                        r3(a_s.ap, 2), pair_view(pmK, WM), AF.Sigmoid, scale=-1.0),
                        reads=[pmK], writes=[a_s])
                    p.emit("scalar", lambda e, pmK=pmK: e.activation(
                        r3(z_s.ap, 2), pair_view(pmK, WM), AF.Sigmoid),
                        reads=[pmK], writes=[z_s])
                    p.emit("scalar", lambda e: e.activation(
                        r3(h_raw.ap, 2), pair_view(pmH, WM), AF.Copy),
                        reads=[pmH], writes=[h_raw])
                    p.emit("scalar", lambda e: e.activation(
                        gt_s.ap, h_raw.ap, AF.Sigmoid),
                        reads=[h_raw], writes=[gt_s])
                    p.emit("vector", lambda e: e.tensor_single_scalar(
                        mk_s.ap, h_raw.ap, 0.0, ALU.is_ge),
                        reads=[h_raw], writes=[mk_s])
                    p.emit("vector", lambda e: e.tensor_single_scalar(
                        v_s.ap, h_raw.ap, 0.5, ALU.add),
                        reads=[h_raw], writes=[v_s])
                    p.emit("vector", lambda e: e.copy_predicated(gt_s.ap, mk_s.ap, v_s.ap),
                           reads=[mk_s, v_s, gt_s], writes=[gt_s])
                    p.emit("vector", lambda e: e.tensor_tensor(v_s.ap, z_s.ap, gt_s.ap,
                                                               ALU.mult),
                           reads=[z_s, gt_s], writes=[v_s])
                    if warm_cols > 0:
                        wc = warm_cols
                        p.emit("vector", lambda e, wc=wc: e.tensor_scalar(
                            r3(a_s.ap, 2)[:, :, 0:wc], r3(a_s.ap, 2)[:, :, 0:wc],
                            smask.ap[:, 0:1], smask.ap[:, 1:2], ALU.mult, ALU.add),
                            reads=[a_s, smask], writes=[a_s])
                        p.emit("vector", lambda e, wc=wc: e.tensor_scalar(
                            r3(v_s.ap, 2)[:, :, 0:wc], r3(v_s.ap, 2)[:, :, 0:wc],
                            smask.ap[:, 0:1], None, ALU.mult),
                            reads=[v_s, smask], writes=[v_s])
                    for half in range(2):
                        c = c0 + half
                        p.emit("vector", lambda e, c=c, half=half: e.tensor_tensor_scan(
                            hseq.ap[:, c * WM:(c + 1) * WM],
                            a_s.ap[:, half * WM:(half + 1) * WM],
                            v_s.ap[:, half * WM:(half + 1) * WM],
                            carry.ap[:, c:c + 1], ALU.mult, ALU.add),
                            reads=[a_s, v_s, carry],
                            writes=[hseq] if c == 11 else ())
                    p.emit("vector", lambda e, c0=c0: e.tensor_copy(
                        hbf.ap[:, c0 * WM:(c0 + 2) * WM],
                        hseq.ap[:, c0 * WM:(c0 + 2) * WM]),
                        reads=[hseq], writes=[hbf])
            def back_dp(j):
                xcur = xbt[j % 3]
                xco = xct[j % 2]
                t0 = j * WM
                p.emit("vector", lambda e: e.tensor_copy(
                    carry.ap.rearrange("p (c o) -> p c o", o=1),
                    hseq.ap.rearrange("p (c w) -> p c w", c=12)[:, :, WM - 1:WM]),
                    reads=[hseq], writes=[carry])

                for k in range(12):
                    for o in range(8):
                        pmD = pmKs[o // 4]
                        po = ((o // 2) % 2) * 512 + (o % 2) * WM
                        p.emit("tensor", lambda e, o=o, k=k, po=po, pmD=pmD: e.matmul(
                            pmD.ap[:, po:po + WM],
                            wdp.ap[:, k * D + o * 128: k * D + o * 128 + 128],
                            hbf.ap[:, k * WM:(k + 1) * WM],
                            start=(k == 0 and o % 2 == 0), stop=(k == 11),
                            skip_group_check=True),
                            reads=[wdp, hbf],
                            writes=[pmD] if (k == 0 and o % 4 == 0) else (),
                            accum=None if (k == 0 and o % 4 == 0) else pmD)
                for op_ in range(4):
                    pmD = pmKs[op_ // 2]
                    po = (op_ % 2) * 512
                    p.emit("vector", lambda e, op_=op_, po=po, xcur=xcur, xco=xco,
                           pmD=pmD: e.tensor_tensor(
                        xco.ap[:, op_ * 2 * WM:(op_ + 1) * 2 * WM],
                        pmD.ap[:, po:po + 2 * WM],
                        xcur.ap[:, op_ * 2 * WM:(op_ + 1) * 2 * WM], ALU.add),
                        reads=[pmD, xcur], writes=[xco])
                p.emit_dma("gpsimd", lambda q, t0=t0, xco=xco: q.dma_start(
                    out=xc_dram[:, :, t0:t0 + WM], in_=r3(xco.ap)),
                    reads=[xco], writes=[xc_d], group=True, lane_buf=xco)

            def back1(j):
                back_q(j, range(0, 3))

            def back2(j):
                back_q(j, range(3, 6))
                back_dp(j)

            for j in range(NT):
                front_a(j)
                if j >= 1:
                    back1(j - 1)
                front_b(j)
                if j >= 1:
                    back2(j - 1)
            back1(NT - 1)
            back2(NT - 1)
            p.barrier()

    if "A2" in phases:
        phase_A2()
    p.barrier()
    stackA.close()

    def phase_B():
        NT = NREAL // WM
        with ExitStack() as stB:
            wm1 = sb("wm1_bf", [128, 8 * HID], BF16, st=stB)
            wm2 = sb("wm2_bf", [128, 32 * D], BF16, st=stB)
            b1eff_row = Buf(rowarena.ap[:, 0:HID], "b1eff_row")
            mb2_row = Buf(rowarena.ap[:, HID:HID + D], "mb2_row")
            psb = ps("psB_sb", st=stB)
            pbb = ps("psB_bb", st=stB)
            pmB = [ps(f"psB_m1{i}", st=stB) for i in range(2)]
            pmm2 = ps("psB_mm2", banks=4, st=stB)

            with ExitStack() as stB0:
                stages = [sb(f"wstgB{i}", [128, 2048], F32, st=stB0) for i in range(2)]
                stages2 = [sb(f"wstgB2{i}", [128, 1024], F32, st=stB0) for i in range(2)]
                gi = sb("gi3col", [128, 16], F32, st=stB0)
                mb1_r = load_row("mlp_b1", "mb1", stB0)
                mb2f = load_row("mlp_b2", "mb2", stB0)
                p.emit("vector", lambda e, mb2f=mb2f: e.tensor_copy(mb2_row.ap, mb2f.ap),
                       reads=[mb2f], writes=[mb2_row])
                gib = sb("gib3", [128, 8], BF16, st=stB0)
                p.claim("vector", wm1)
                for k in range(8):
                    for h2 in range(2):
                        stg = stages[(k * 2 + h2) % 2]
                        p.emit_dma("scalar", lambda q, k=k, h2=h2, stg=stg: q.dma_start(
                            out=stg.ap, in_=wm1d[:, k, h2 * 2048:(h2 + 1) * 2048]),
                            writes=[stg])
                        p.emit("vector", lambda e, k=k, h2=h2, stg=stg: e.tensor_scalar(
                            wm1.ap[:, k * HID + h2 * 2048: k * HID + (h2 + 1) * 2048],
                            stg.ap, col("ln3_g", k), None, ALU.mult),
                            reads=[stg, pc],
                            writes=[wm1] if (k == 7 and h2 == 1) else ())
                p.claim("vector", wm2)
                for k in range(32):
                    stg = stages2[k % 2]
                    p.emit_dma("sync", lambda q, k=k, stg=stg: q.dma_start(
                        out=stg.ap[:, 0:D], in_=wm2d[:, k, :]), writes=[stg])
                    p.emit("vector", lambda e, k=k, stg=stg: e.tensor_copy(
                        wm2.ap[:, k * D:(k + 1) * D], stg.ap[:, 0:D]),
                        reads=[stg], writes=[wm2] if k == 31 else ())
                p.emit("vector", lambda e, gi=gi: e.reciprocal(
                    gi.ap[:, 0:8], col("ln3_g", 0, 8)), reads=[pc], writes=[gi])
                p.emit("vector", lambda e, gi=gi: e.tensor_tensor(
                    gi.ap[:, 8:16], col("ln3_b", 0, 8), gi.ap[:, 0:8], ALU.mult),
                    reads=[pc, gi], writes=[gi])
                p.emit("vector", lambda e, gi=gi, gib=gib: e.tensor_copy(
                    gib.ap, gi.ap[:, 8:16]), reads=[gi], writes=[gib])
                p.claim("vector", b1eff_row)
                for cc in range(8):
                    for k in range(8):
                        p.emit("tensor", lambda e, cc=cc, k=k, gib=gib: e.matmul(
                            psb.ap[0:1, 0:512], gib.ap[:, k:k + 1],
                            wm1.ap[:, k * HID + cc * 512: k * HID + cc * 512 + 512],
                            start=(k == 0), stop=(k == 7)),
                            reads=[gib, wm1], writes=[psb] if k == 0 else (),
                            accum=None if k == 0 else psb)
                    p.emit("vector", lambda e, cc=cc, mb1_r=mb1_r:
                           e.scalar_tensor_tensor(
                        b1eff_row.ap[:, cc * 512:(cc + 1) * 512], psb.ap[0:1, 0:512], 1.0,
                        mb1_r.ap[:, cc * 512:(cc + 1) * 512], ALU.mult, ALU.add),
                        reads=[psb, mb1_r], writes=[b1eff_row] if cc == 7 else ())
                p.barrier()

            xctB = [sb(f"xctB_{i}", [128, 8 * WM], st=stB) for i in range(2)]
            twb = sb("twB", [128, 8 * WM], st=stB)
            sqbfB = sb("sqbfB", [128, 8 * WM], BF16, st=stB)
            g3bf = [sb(f"g3bfB_{i}", [128, 8 * WM], BF16, st=stB) for i in range(2)]
            hstrip = sb("hstripB", [128, 8 * WM], BF16, st=stB)
            xdB = sb("xdB", [128, 8 * WM], st=stB)

            def front_a(j):
                xcur = xctB[j % 2]
                rows = rows2[j % 2]
                gcur = g3bf[j % 2]
                t0 = WARM + j * WM
                p.emit_dma("sync", lambda q, t0=t0, xcur=xcur: q.dma_start(
                    out=r3(xcur.ap), in_=xc_dram[:, :, t0:t0 + WM]),
                    reads=[xc_d], writes=[xcur])
                ln_stats(rows, xcur, WM, 8, gcur, sqbfB, psb)

            def front_b(j):
                xcur = xctB[j % 2]
                rows = rows2[j % 2]
                gcur = g3bf[j % 2]
                bcast(rows, "negm", WM, pbb, 0, True)
                bcast(rows, "inv", WM, pbb, WM, False)
                apply2(xcur, twb, WM, 8, pbb, 0, pbb, WM)
                p.emit("scalar", lambda e, gcur=gcur: e.activation(
                    gcur.ap, twb.ap, AF.Copy), reads=[twb], writes=[gcur])

            def back_mm(j, hcs):
                gcur = g3bf[j % 2]
                for hc in hcs:
                    p.claim("scalar", hstrip)
                    for s_ in range(8):
                        hidx = hc * 8 + s_
                        bank = pmB[s_ % 2]
                        p.emit("tensor", lambda e, hidx=hidx, bank=bank: e.matmul(
                            bank.ap[:, 0:WM],
                            b1eff_row.ap[:, hidx * 128:(hidx + 1) * 128],
                            onesw_bf.ap[:, 0:WM], start=True, stop=False, skip_group_check=True),
                            reads=[b1eff_row, onesw_bf], writes=[bank])
                        for k in range(8):
                            p.emit("tensor", lambda e, hidx=hidx, k=k, bank=bank,
                                   gcur=gcur: e.matmul(
                                bank.ap[:, 0:WM],
                                wm1.ap[:, k * HID + hidx * 128: k * HID + hidx * 128 + 128],
                                gcur.ap[:, k * WM:(k + 1) * WM],
                                start=False, stop=(k == 7), skip_group_check=True),
                                reads=[wm1, gcur], writes=(), accum=bank)
                        p.emit("scalar", lambda e, s_=s_, bank=bank: e.activation(
                            hstrip.ap[:, s_ * WM:(s_ + 1) * WM], bank.ap[:, 0:WM],
                            AF.Gelu),
                            reads=[bank], writes=[hstrip] if s_ == 7 else ())
                    for o in range(8):
                        if hc == 0:
                            p.emit("tensor", lambda e, o=o: e.matmul(
                                pmm2.ap[:, o * WM:(o + 1) * WM],
                                mb2_row.ap[:, o * 128:(o + 1) * 128], onesw_bf.ap[:, 0:WM],
                                start=(o % 2 == 0), stop=False, skip_group_check=True),
                                reads=[mb2_row, onesw_bf],
                                writes=[pmm2] if o == 0 else (),
                                accum=None if o == 0 else pmm2)
                        for s in range(8):
                            hidx = hc * 8 + s
                            p.emit("tensor", lambda e, o=o, s=s, hidx=hidx: e.matmul(
                                pmm2.ap[:, o * WM:(o + 1) * WM],
                                wm2.ap[:, hidx * D + o * 128: hidx * D + o * 128 + 128],
                                hstrip.ap[:, s * WM:(s + 1) * WM],
                                start=False, stop=(hc == 3 and s == 7),
                                skip_group_check=True),
                                reads=[wm2, hstrip], writes=(), accum=pmm2)
            def back_ln(j):
                xcur = xctB[j % 2]
                rows = rows2[j % 2]
                gcur = g3bf[j % 2]
                t0 = WARM + j * WM
                p.claim("vector", xdB)
                for op_ in range(4):
                    p.emit("vector", lambda e, op_=op_, xcur=xcur: e.tensor_tensor(
                        xdB.ap[:, op_ * 2 * WM:(op_ + 1) * 2 * WM],
                        pmm2.ap[:, op_ * 2 * WM:(op_ + 1) * 2 * WM],
                        xcur.ap[:, op_ * 2 * WM:(op_ + 1) * 2 * WM], ALU.add),
                        reads=[pmm2, xcur], writes=[xdB] if op_ == 3 else ())

                ln_stats(rows, xdB, WM, 8, gcur, sqbfB, psb)
                bcast(rows, "negm", WM, pbb, 0, True)
                bcast(rows, "inv", WM, pbb, WM, False)
                apply2(xdB, twb, WM, 8, pbb, 0, pbb, WM)
                p.emit_dma("scalar", lambda q, t0=t0, xdB=xdB: q.dma_start(
                    out=r3(xdB.ap), in_=xd[:, :, t0:t0 + WM]),
                    reads=[x_in], writes=[xdB])
                p.emit("vector", lambda e, xdB=xdB: e.tensor_tensor(
                    xdB.ap, xdB.ap, twb.ap, ALU.add),
                    reads=[xdB, twb], writes=[xdB])
                p.emit_dma("gpsimd", lambda q, j=j, xdB=xdB: q.dma_start(
                    out=xm_dram[:, :, j * WM:(j + 1) * WM], in_=r3(xdB.ap)),
                    reads=[xdB], writes=[xm_d], group=True, lane_buf=xdB)

            def back1(j):
                back_mm(j, range(0, 2))

            def back2(j):
                back_mm(j, range(2, 4))
                back_ln(j)

            for j in range(NT):
                front_a(j)
                if j >= 1:
                    back1(j - 1)
                front_b(j)
                if j >= 1:
                    back2(j - 1)
            back1(NT - 1)
            back2(NT - 1)
            p.barrier()

    if "B" in phases:
        phase_B()

    def quantize(dram_ap, nk, ncols, dst_bf, g_col, st, tag):
        mcol = sb(f"wq_mcol_{tag}", [128, 1], F32, st=st)
        mrow = sb(f"wq_mrow_{tag}", [1, 1], F32, st=st)
        with ExitStack() as stq:
            stages = [sb(f"wq_stage_{tag}{i}", [128, ncols], F32, st=stq) for i in range(2)]
            q1 = sb(f"wq_q1_{tag}", [128, ncols], F32, st=stq)
            q2 = sb(f"wq_q2_{tag}", [128, ncols], F32, st=stq)
            asum = sb(f"wq_asum_{tag}", [128, nk], F32, st=stq)
            onecol = sb(f"wq_one_{tag}", [128, 1], F32, st=stq)
            thrcol = sb(f"wq_thr_{tag}", [128, 2], F32, st=stq)
            pq = ps(f"wq_ps_{tag}", st=stq)
            p.emit("vector", lambda e, onecol=onecol: e.memset(onecol.ap, 1.0),
                   writes=[onecol])
            p.claim("vector", asum)
            for k in range(nk):
                stg = stages[k % 2]
                p.emit_dma("sync", lambda q, k=k, stg=stg, dram_ap=dram_ap: q.dma_start(
                    out=stg.ap, in_=dram_ap[:, k, :]), writes=[stg])
                p.emit("scalar", lambda e, stg=stg, q1=q1: e.activation(
                    q1.ap, stg.ap, AF.Abs), reads=[stg], writes=[q1])
                p.emit("vector", lambda e, k=k, q1=q1, asum=asum: e.tensor_reduce(
                    asum.ap[:, k:k + 1], q1.ap, mybir.AxisListType.X, ALU.add),
                    reads=[q1], writes=[asum] if k == nk - 1 else ())
            p.emit("tensor", lambda e, onecol=onecol, asum=asum, pq=pq, nk=nk: e.matmul(
                pq.ap[0:1, 0:nk], onecol.ap, asum.ap[:, 0:nk], start=True, stop=True),
                reads=[onecol, asum], writes=[pq])
            p.emit("vector", lambda e, pq=pq, nk=nk: e.tensor_reduce(
                rq["tot"].ap[:, 0:1], pq.ap[0:1, 0:nk], mybir.AxisListType.X, ALU.add),
                reads=[pq], writes=[rq["tot"]])
            p.emit("vector", lambda e, nk=nk, ncols=ncols: e.tensor_single_scalar(
                rq["m1"].ap[:, 0:1], rq["tot"].ap[:, 0:1], 1.0 / (nk * 128 * ncols),
                ALU.mult), reads=[rq["tot"]], writes=[rq["m1"]])
            p.emit("vector", lambda e: e.tensor_single_scalar(
                rq["m1"].ap[:, 0:1], rq["m1"].ap[:, 0:1], 1e-5, ALU.max),
                reads=[rq["m1"]], writes=[rq["m1"]])
            p.emit("vector", lambda e, mrow=mrow: e.tensor_copy(
                mrow.ap, rq["m1"].ap[:, 0:1]), reads=[rq["m1"]], writes=[mrow])
            p.emit("vector", lambda e: e.tensor_single_scalar(
                rq["thr"].ap[:, 0:1], rq["m1"].ap[:, 0:1], 0.5, ALU.mult),
                reads=[rq["m1"]], writes=[rq["thr"]])
            p.emit("tensor", lambda e, pq=pq: e.matmul(
                pq.ap[:, 0:1], onesrow.ap, rq["m1"].ap[:, 0:1], start=True, stop=True),
                reads=[onesrow, rq["m1"]], writes=[pq])
            p.emit("vector", lambda e, pq=pq, mcol=mcol: e.tensor_copy(
                mcol.ap, pq.ap[:, 0:1]), reads=[pq], writes=[mcol])
            p.emit("tensor", lambda e, pq=pq: e.matmul(
                pq.ap[:, 0:1], onesrow.ap, rq["thr"].ap[:, 0:1], start=True, stop=True),
                reads=[onesrow, rq["thr"]], writes=[pq])
            p.emit("vector", lambda e, pq=pq, thrcol=thrcol: e.tensor_copy(
                thrcol.ap[:, 0:1], pq.ap[:, 0:1]), reads=[pq], writes=[thrcol])
            p.emit("vector", lambda e, thrcol=thrcol: e.tensor_single_scalar(
                thrcol.ap[:, 1:2], thrcol.ap[:, 0:1], -1.0, ALU.mult),
                reads=[thrcol], writes=[thrcol])
            p.claim("vector", dst_bf)
            for k in range(nk):
                stg = stages[k % 2]
                p.emit_dma("sync", lambda q, k=k, stg=stg, dram_ap=dram_ap: q.dma_start(
                    out=stg.ap, in_=dram_ap[:, k, :]), writes=[stg])
                p.emit("vector", lambda e, stg=stg, q1=q1, thrcol=thrcol: e.tensor_scalar(
                    q1.ap, stg.ap, thrcol.ap[:, 0:1], None, ALU.is_gt),
                    reads=[stg, thrcol], writes=[q1])
                p.emit("vector", lambda e, stg=stg, q2=q2, thrcol=thrcol: e.tensor_scalar(
                    q2.ap, stg.ap, thrcol.ap[:, 1:2], None, ALU.is_lt),
                    reads=[stg, thrcol], writes=[q2])
                p.emit("vector", lambda e, q1=q1, q2=q2: e.tensor_tensor(
                    q1.ap, q1.ap, q2.ap, ALU.subtract),
                    reads=[q1, q2], writes=[q1])
                p.emit("vector", lambda e, k=k, q1=q1, dst_bf=dst_bf, ncols=ncols,
                       sc=g_col(k): e.tensor_scalar(
                           dst_bf.ap[:, k * ncols:(k + 1) * ncols], q1.ap, sc, None,
                           ALU.mult),
                       reads=[q1, pc], writes=[dst_bf] if k == nk - 1 else ())
            p.barrier()
        return mcol, mrow

    def phase_C1():
        NT = NREAL // WM
        with ExitStack() as stC:
            w1s = sb("w1s_bf", [128, 8 * HID], BF16, st=stC)
            m1col, m1row = quantize(ww1d, 8, HID, w1s, lambda k: col("fln1_g", k),
                                    stC, "w1")
            b1f_row = Buf(rowarena.ap[:, 0:HID], "b1f_row")
            psb = ps("psC1_sb", st=stC)
            pbb = ps("psC1_bb", st=stC)
            psbH = ps("psC1_sbH", st=stC)
            pbbH = ps("psC1_bbH", st=stC)
            pmC = [ps(f"psC1_m{i}", banks=2, st=stC) for i in range(2)]

            with ExitStack() as stC0:
                gi = sb("giC1", [128, 16], F32, st=stC0)
                fb1_r = load_row("fb1", "fb1", stC0)
                gib = sb("gibC1", [128, 8], BF16, st=stC0)
                brow = sb("browC1", [1, HID], F32, st=stC0)
                p.emit("vector", lambda e, gi=gi: e.reciprocal(
                    gi.ap[:, 0:8], col("fln1_g", 0, 8)), reads=[pc], writes=[gi])
                p.emit("vector", lambda e, gi=gi: e.tensor_tensor(
                    gi.ap[:, 8:16], col("fln1_b", 0, 8), gi.ap[:, 0:8], ALU.mult),
                    reads=[pc, gi], writes=[gi])
                p.emit("vector", lambda e, gi=gi, gib=gib: e.tensor_copy(
                    gib.ap, gi.ap[:, 8:16]), reads=[gi], writes=[gib])
                p.claim("vector", brow)
                for cc in range(8):
                    for k in range(8):
                        p.emit("tensor", lambda e, cc=cc, k=k, gib=gib: e.matmul(
                            psb.ap[0:1, 0:512], gib.ap[:, k:k + 1],
                            w1s.ap[:, k * HID + cc * 512: k * HID + cc * 512 + 512],
                            start=(k == 0), stop=(k == 7)),
                            reads=[gib, w1s], writes=[psb] if k == 0 else (),
                            accum=None if k == 0 else psb)
                    p.emit("vector", lambda e, cc=cc, brow=brow, fb1_r=fb1_r:
                           e.scalar_tensor_tensor(
                        brow.ap[:, cc * 512:(cc + 1) * 512], psb.ap[0:1, 0:512],
                        m1row.ap[0:1, 0:1],
                        fb1_r.ap[:, cc * 512:(cc + 1) * 512], ALU.mult, ALU.add),
                        reads=[psb, m1row, fb1_r], writes=[brow] if cc == 7 else ())
                p.emit("vector", lambda e, brow=brow: e.tensor_copy(b1f_row.ap, brow.ap),
                       reads=[brow], writes=[b1f_row])
                p.barrier()

            xmt = [sb(f"xmtC1_{i}", [128, 8 * WM], st=stC) for i in range(2)]
            twc = sb("twC1", [128, 8 * WM], st=stC)
            sqbfC = sb("sqbfC1", [128, 8 * WM], BF16, st=stC)
            zlnbf = [sb(f"zlnbfC1_{i}", [128, 8 * WM], BF16, st=stC) for i in range(2)]
            hC = sb("hC1", [128, 32 * WM], st=stC)
            hbfC = sb("hbfC1", [128, 32 * WM], BF16, st=stC)
            hsqC = sb("hsqC1", [128, 32 * WM], BF16, st=stC)
            hnst = sb("hnstC1", [128, 8 * WM], BF16, st=stC)

            def front_a(j):
                xcur = xmt[j % 2]
                rows = rows2[j % 2]
                zcur = zlnbf[j % 2]
                p.emit_dma("sync", lambda q, j=j, xcur=xcur: q.dma_start(
                    out=r3(xcur.ap), in_=xm_dram[:, :, j * WM:(j + 1) * WM]),
                    reads=[xm_d], writes=[xcur])
                ln_stats(rows, xcur, WM, 8, zcur, sqbfC, psb, composed=True)

            def front_b(j):
                xcur = xmt[j % 2]
                rows = rows2[j % 2]
                zcur = zlnbf[j % 2]
                bcast(rows, "negm", WM, pbb, 0, True)
                bcast(rows, "inv12", WM, pbb, WM, False)
                apply2(xcur, twc, WM, 8, pbb, 0, pbb, WM)
                p.emit("vector", lambda e, zcur=zcur: e.tensor_scalar(
                    zcur.ap, twc.ap, m1col.ap[:, 0:1], None, ALU.mult),
                    reads=[twc, m1col], writes=[zcur])

            def mm_group(j, op_):
                    zcur = zlnbf[j % 2]
                    bank = pmC[op_ % 2]
                    for half in range(2):
                        o = op_ * 2 + half
                        po = half * 512
                        p.emit("tensor", lambda e, o=o, po=po, bank=bank: e.matmul(
                            bank.ap[:, po:po + WM], b1f_row.ap[:, o * 128:(o + 1) * 128],
                            onesw_bf.ap[:, 0:WM], start=True, stop=False, skip_group_check=True),
                            reads=[b1f_row, onesw_bf],
                            writes=[bank] if half == 0 else (),
                            accum=None if half == 0 else bank)
                        for k in range(8):
                            p.emit("tensor", lambda e, o=o, k=k, po=po, bank=bank,
                                   zcur=zcur: e.matmul(
                                bank.ap[:, po:po + WM],
                                w1s.ap[:, k * HID + o * 128: k * HID + o * 128 + 128],
                                zcur.ap[:, k * WM:(k + 1) * WM],
                                start=False, stop=(k == 7), skip_group_check=True),
                                reads=[w1s, zcur], writes=(), accum=bank)
                    p.emit("scalar", lambda e, op_=op_, bank=bank: e.activation(
                        hC.ap[:, op_ * 2 * WM:(op_ + 1) * 2 * WM].rearrange(
                            "p (b w) -> p b w", b=2), pair_view(bank, WM), AF.Gelu),
                        reads=[bank], writes=[hC])

            def stats_block(hc):
                p.emit("vector", lambda e, hc=hc: e.tensor_copy(
                    hbfC.ap[:, hc * 8 * WM:(hc + 1) * 8 * WM],
                    hC.ap[:, hc * 8 * WM:(hc + 1) * 8 * WM]),
                    reads=[hC], writes=[hbfC])
                p.emit("scalar", lambda e, hc=hc: e.activation(
                    hsqC.ap[:, hc * 8 * WM:(hc + 1) * 8 * WM],
                    hC.ap[:, hc * 8 * WM:(hc + 1) * 8 * WM], AF.Square),
                    reads=[hC], writes=[hsqC])
                for h2 in range(8):
                    k = hc * 8 + h2
                    p.emit("tensor", lambda e, k=k: e.matmul(
                        psbH.ap[0:1, 0:WM], ones_bf.ap,
                        hbfC.ap[:, k * WM:(k + 1) * WM],
                        start=(k == 0), stop=(k == 31), skip_group_check=True),
                        reads=[ones_bf, hbfC],
                        writes=[psbH] if k == 0 else (),
                        accum=None if k == 0 else psbH)
                for h2 in range(8):
                    k = hc * 8 + h2
                    p.emit("tensor", lambda e, k=k: e.matmul(
                        psbH.ap[0:1, WM:2 * WM], ones_bf.ap,
                        hsqC.ap[:, k * WM:(k + 1) * WM],
                        start=False, stop=(k == 31), skip_group_check=True),
                        reads=[ones_bf, hsqC], writes=(), accum=psbH)

            def back_pre(j):
                p.claim("scalar", hC)
                for hc in range(0, 2):
                    for op_ in range(hc * 4, hc * 4 + 4):
                        mm_group(j, op_)
                    if hc >= 1:
                        stats_block(hc - 1)

            def back_post(j):
                rows = rows2[j % 2]
                for hc in range(2, 4):
                    for op_ in range(hc * 4, hc * 4 + 4):
                        mm_group(j, op_)
                    stats_block(hc - 1)
                stats_block(3)

                row_chain(rows, psbH, psbH, WM, WM, 4096, False)
                bcast(rows, "negm", WM, pbbH, 0, True)
                bcast(rows, "inv", WM, pbbH, WM, False)
                for hc in range(4):
                    p.emit("vector", lambda e, hc=hc: e.tensor_tensor(
                        r3(twc.ap), r3(hC.ap[:, hc * 8 * WM:(hc + 1) * 8 * WM]),
                        pbbH.ap[:, 0:WM].rearrange("p (o w) -> p o w", o=1).broadcast_to(
                            (128, 8, WM)), ALU.add),
                        reads=[hC, pbbH], writes=[twc])
                    p.emit("vector", lambda e: e.tensor_tensor(
                        r3(twc.ap), r3(twc.ap),
                        pbbH.ap[:, WM:2 * WM].rearrange("p (o w) -> p o w", o=1).broadcast_to(
                            (128, 8, WM)), ALU.mult),
                        reads=[twc, pbbH], writes=[twc])
                    p.emit("scalar", lambda e: e.activation(hnst.ap, twc.ap, AF.Copy),
                           reads=[twc], writes=[hnst])
                    p.emit_dma("gpsimd", lambda q, j=j, hc=hc: q.dma_start(
                        out=hn_dram[:, hc * 8:(hc + 1) * 8, j * WM:(j + 1) * WM],
                        in_=r3(hnst.ap)),
                        reads=[hnst], writes=[hn_d], group=True, lane_buf=hnst)

            def back1(j):
                back_pre(j)

            def back2(j):
                back_post(j)

            for j in range(NT):
                front_a(j)
                if j >= 1:
                    back1(j - 1)
                front_b(j)
                if j >= 1:
                    back2(j - 1)
            back1(NT - 1)
            back2(NT - 1)
            p.barrier()

    if "C1" in phases:
        phase_C1()

    def phase_C2():
        NT = NREAL // WM
        with ExitStack() as stC:
            w2s = sb("w2s_bf", [128, 32 * D], BF16, st=stC)
            m2col, m2row = quantize(ww2d, 32, D, w2s, lambda k: col("fln2_g", k),
                                    stC, "w2")
            c2eff_row = Buf(rowarena.ap[:, HID:HID + D], "c2eff_row")
            psb = ps("psC2_sb", st=stC)
            pbb = ps("psC2_bb", st=stC)
            pmD = [ps(f"psC2_m{i}", banks=2, st=stC) for i in range(2)]

            with ExitStack() as stC0:
                gi = sb("giC2", [128, 64], F32, st=stC0)
                fb2_r = load_row("fb2", "fb2", stC0)
                gib = sb("gibC2", [128, 32], BF16, st=stC0)
                m2inv = sb("m2inv", [1, 1], F32, st=stC0)
                brow = sb("browC2", [1, D], F32, st=stC0)
                p.emit("vector", lambda e, gi=gi: e.reciprocal(
                    gi.ap[:, 0:32], col("fln2_g", 0, 32)), reads=[pc], writes=[gi])
                p.emit("vector", lambda e, gi=gi: e.tensor_tensor(
                    gi.ap[:, 32:64], col("fln2_b", 0, 32), gi.ap[:, 0:32], ALU.mult),
                    reads=[pc, gi], writes=[gi])
                p.emit("vector", lambda e, gi=gi, gib=gib: e.tensor_copy(
                    gib.ap, gi.ap[:, 32:64]), reads=[gi], writes=[gib])
                p.claim("vector", brow)
                for cc in range(2):
                    for k in range(32):
                        p.emit("tensor", lambda e, cc=cc, k=k, gib=gib: e.matmul(
                            psb.ap[0:1, 0:512], gib.ap[:, k:k + 1],
                            w2s.ap[:, k * D + cc * 512: k * D + cc * 512 + 512],
                            start=(k == 0), stop=(k == 31)),
                            reads=[gib, w2s], writes=[psb] if k == 0 else (),
                            accum=None if k == 0 else psb)
                    p.emit("vector", lambda e, cc=cc, brow=brow: e.tensor_copy(
                        brow.ap[:, cc * 512:(cc + 1) * 512], psb.ap[0:1, 0:512]),
                        reads=[psb], writes=[brow] if cc == 1 else ())
                p.emit("vector", lambda e, m2inv=m2inv: e.reciprocal(
                    m2inv.ap, m2row.ap), reads=[m2row], writes=[m2inv])
                p.emit("vector", lambda e, brow=brow, m2inv=m2inv, fb2_r=fb2_r:
                       e.scalar_tensor_tensor(
                    brow.ap[:, 0:D], fb2_r.ap, m2inv.ap, brow.ap[:, 0:D],
                    ALU.mult, ALU.add),
                    reads=[fb2_r, m2inv, brow], writes=[brow])
                p.emit("vector", lambda e, brow=brow: e.tensor_copy(c2eff_row.ap,
                                                                    brow.ap[:, 0:D]),
                       reads=[brow], writes=[c2eff_row])
                p.barrier()

            hnt = [sb(f"hntC2_{i}", [128, 32 * WM], BF16, st=stC) for i in range(2)]
            xmt2 = [sb(f"xmtC2_{i}", [128, 8 * WM], st=stC) for i in range(3)]
            twd = sb("twC2", [128, 8 * WM], st=stC)
            xbfD = sb("xbfC2", [128, 8 * WM], BF16, st=stC)
            sqbfD = sb("sqbfC2", [128, 8 * WM], BF16, st=stC)
            yfC = [sb(f"yfC2_{i}", [128, 8 * WM], st=stC) for i in range(3)]

            def core(j):
                hcur = hnt[j % 2]
                xcur = xmt2[j % 3]
                ycur = yfC[j % 3]
                p.emit_dma("sync", lambda q, j=j, hcur=hcur: q.dma_start(
                    out=r3(hcur.ap, 32), in_=hn_dram[:, :, j * WM:(j + 1) * WM]),
                    reads=[hn_d], writes=[hcur])
                p.emit_dma("scalar", lambda q, j=j, xcur=xcur: q.dma_start(
                    out=r3(xcur.ap), in_=xm_dram[:, :, j * WM:(j + 1) * WM]),
                    reads=[xm_d], writes=[xcur])
                p.claim("vector", ycur)
                for op_ in range(4):
                    bank = pmD[op_ % 2]
                    for half in range(2):
                        o = op_ * 2 + half
                        po = half * 512
                        p.emit("tensor", lambda e, o=o, po=po, bank=bank: e.matmul(
                            bank.ap[:, po:po + WM], c2eff_row.ap[:, o * 128:(o + 1) * 128],
                            onesw_bf.ap[:, 0:WM], start=True, stop=False, skip_group_check=True),
                            reads=[c2eff_row, onesw_bf],
                            writes=[bank] if half == 0 else (),
                            accum=None if half == 0 else bank)
                        for k in range(32):
                            p.emit("tensor", lambda e, o=o, k=k, po=po, bank=bank,
                                   hcur=hcur: e.matmul(
                                bank.ap[:, po:po + WM],
                                w2s.ap[:, k * D + o * 128: k * D + o * 128 + 128],
                                hcur.ap[:, k * WM:(k + 1) * WM],
                                start=False, stop=(k == 31), skip_group_check=True),
                                reads=[w2s, hcur], writes=(), accum=bank)
                    p.emit("vector", lambda e, op_=op_, bank=bank, ycur=ycur: e.tensor_scalar(
                        r3(ycur.ap[:, op_ * 2 * WM:(op_ + 1) * 2 * WM], 2),
                        pair_view(bank, WM), m2col.ap[:, 0:1], None, ALU.mult),
                        reads=[bank, m2col], writes=[ycur] if op_ == 3 else ())

            def finish_a(j):
                ycur = yfC[j % 3]
                rows = rows2[j % 2]
                ln_stats(rows, ycur, WM, 8, xbfD, sqbfD, psb)

            def finish_b(j):
                xcur = xmt2[j % 3]
                ycur = yfC[j % 3]
                rows = rows2[j % 2]
                bcast(rows, "negm", WM, pbb, 0, True)
                bcast(rows, "inv", WM, pbb, WM, False)
                apply2(ycur, twd, WM, 8, pbb, 0, pbb, WM)
                p.emit("vector", lambda e, xcur=xcur, ycur=ycur: e.tensor_tensor(
                    ycur.ap, xcur.ap, twd.ap, ALU.add),
                    reads=[xcur, twd], writes=[ycur])
                p.emit_dma("gpsimd", lambda q, j=j, ycur=ycur: q.dma_start(
                    out=yd[:, :, j * WM:(j + 1) * WM], in_=r3(ycur.ap)),
                    reads=[ycur], writes=[y_out], group=True, lane_buf=ycur)

            for j in range(NT):
                core(j)
                if j >= 1:
                    finish_a(j - 1)
                if j >= 2:
                    finish_b(j - 2)
            finish_a(NT - 1)
            finish_b(NT - 2)
            finish_b(NT - 1)

    if "C2" in phases:
        phase_C2()

    for lane, v in p.lane_val.items():
        if v > 0:
            p.wait_ref("gpsimd", ("lane", lane, v))

    with nc.Block() as block:
        p.finalize(block)
    stack.close()
    return nc, p



def _pack_pcols(ins):
    cols = np.zeros((128, PARAM_COLS), np.float32)

    def put(nm, arr):
        o = PARAM_OFF[nm]
        c = _cols(arr)
        cols[:, o:o + c.shape[1]] = c

    put("pw_b", ins["L0_attn_pw_b"])
    put("dw_b", ins["L0_attn_dw_b"])
    put("ln1_g", ins["L0_attn_ln1_g"])
    put("ln1_b", ins["L0_attn_ln1_b"])
    put("ln2_g", ins["L0_attn_ln2_g"])
    put("ln2_b", ins["L0_attn_ln2_b"])
    put("ln3_g", ins["L0_attn_ln3_g"])
    put("ln3_b", ins["L0_attn_ln3_b"])
    put("mlp_b2", ins["L0_attn_mlp_b2"])
    put("fb2", ins["L0_ffn_b2"])
    put("fln1_g", ins["L0_ffn_ln1_g"])
    put("fln1_b", ins["L0_ffn_ln1_b"])
    put("f_b", ins["L0_attn_f_b"])
    put("mlp_b1", ins["L0_attn_mlp_b1"])
    put("fb1", ins["L0_ffn_b1"])
    put("fln2_g", ins["L0_ffn_ln2_g"])
    put("fln2_b", ins["L0_ffn_ln2_b"])
    dw = np.asarray(ins["L0_attn_dw_w"], np.float32)[:, 0, :]
    dwt = dw.reshape(8, 128, 4).transpose(1, 0, 2).reshape(128, 32)
    cols[:, PARAM_OFF["dwtaps"]:PARAM_OFF["dwtaps"] + 32] = dwt
    return np.ascontiguousarray(cols)


def _pack_prows(ins):
    rows = np.zeros((1, PROW_COLS), np.float32)

    def put(nm, arr):
        o = PROW_OFF[nm]
        a = np.asarray(arr, np.float32).reshape(-1)
        rows[0, o:o + a.size] = a

    put("f_b", ins["L0_attn_f_b"])
    put("pw_b", ins["L0_attn_pw_b"])
    put("mlp_b1", ins["L0_attn_mlp_b1"])
    put("mlp_b2", ins["L0_attn_mlp_b2"])
    put("fb1", ins["L0_ffn_b1"])
    put("fb2", ins["L0_ffn_b2"])
    put("dw_b", ins["L0_attn_dw_b"])
    put("ln1_b", ins["L0_attn_ln1_b"])
    put("dwtaps", np.asarray(ins["L0_attn_dw_w"], np.float32)[:, 0, :].reshape(-1))
    return np.ascontiguousarray(rows)


def kernel(**ins):
    x = np.asarray(ins["L0_attn_x"], dtype=np.float32)

    shared = {
        "pcols": _pack_pcols(ins),
        "prows": _pack_prows(ins),
        "ident": np.eye(128, dtype=np.float32),
        "onesrow": np.ones((1, 128), np.float32),
        "wpw": _tiles(np.asarray(ins["L0_attn_pw_w"]).T),
        "wfw": _tiles(np.asarray(ins["L0_attn_f_w"]).T),
        "wdp": _tiles(np.asarray(ins["L0_attn_dp_w"]).T),
        "wm1": _tiles(np.asarray(ins["L0_attn_mlp_w1"]).T),
        "wm2": _tiles(np.asarray(ins["L0_attn_mlp_w2"]).T),
        "ww1": _tiles(np.asarray(ins["L0_ffn_w1"]).T),
        "ww2": _tiles(np.asarray(ins["L0_ffn_w2"]).T),
    }

    def shard_x(core):
        b, half = core // 2, core % 2
        if half == 0:
            s = np.concatenate([np.zeros((WARM, D), np.float32), x[b, 0:NREAL]], 0)
        else:
            s = x[b, NREAL - WARM: 2 * NREAL]
        return np.ascontiguousarray(s.T.reshape(8, 128, T).transpose(1, 0, 2))

    def shard_mask(core):
        half = core % 2
        m = np.zeros((128, 2), np.float32)
        if half == 0:
            m[:, 1] = 1.0
        else:
            m[:, 0] = 1.0
        return m

    in_maps = []
    for core in range(8):
        d = dict(shared)
        d["x"] = shard_x(core)
        d["smask"] = shard_mask(core)
        in_maps.append(d)

    nc, _ = build_program()
    res = run_bass_kernel_spmd(nc, in_maps, core_ids=list(range(8)))

    out = np.empty((4, 2 * NREAL, D), np.float32)
    for core in range(8):
        b, half = core // 2, core % 2
        yk = res.results[core]["y"]
        out[b, half * NREAL:(half + 1) * NREAL] = yk.transpose(2, 1, 0).reshape(NREAL, D)
    return out


if __name__ == "__main__":
    nc_, p_ = build_program()
    import json
    print(json.dumps(p_.stats(), indent=1))


# version 16
# speedup vs baseline: 1.1829x; 1.1829x over previous
from collections import defaultdict
from contextlib import ExitStack

import numpy as np
import concourse.bass as bass
import concourse.mybir as mybir
from concourse.bass_utils import run_bass_kernel_spmd

F32 = mybir.dt.float32
BF16 = mybir.dt.bfloat16
U8 = mybir.dt.uint8
AF = mybir.ActivationFunctionType
ALU = mybir.AluOpType

D = 1024
EXP = 1536
HID = 4096
NREAL = 4096
WARM = 512
T = NREAL + WARM
WA = 288
MASK_U8 = True
A2_STOP = 99
WM = 256
EPS = 1e-5

ENGINES = ("tensor", "vector", "scalar", "gpsimd", "sync")



class Buf:
    __slots__ = ("name", "ap", "writer", "readers", "lanes", "is_psum")

    def __init__(self, ap, name=""):
        self.ap = ap
        self.name = name
        self.writer = None
        self.readers = {}
        self.lanes = {}
        self.is_psum = False


class _Op:
    __slots__ = ("kind", "payload", "need_inc", "value", "lane")

    def __init__(self, kind, payload, lane=None):
        self.kind = kind
        self.payload = payload
        self.need_inc = False
        self.value = 0
        self.lane = lane


class Prog:
    def __init__(self, nc):
        self.nc = nc
        self.ops = {e: [] for e in ENGINES}
        self.sems = {}
        self._sem_ctxs = []
        self._lane_n = 0
        self.lane_val = {}
        for e in ENGINES:
            self._make_sem(e)

    def _make_sem(self, name):
        ctx = self.nc.semaphore(f"S_{name}")
        self.sems[name] = ctx.__enter__()
        self._sem_ctxs.append(ctx)
        return name

    def new_lane(self, tag=""):
        self._lane_n += 1
        return self._make_sem(f"L{self._lane_n}_{tag}")

    def _edge(self, consumer, ref):
        if ref is None:
            return
        if ref[0] == "op":
            _, pe, pidx = ref
            self.ops[pe][pidx].need_inc = True
        self.ops[consumer].append(_Op("wait", ref))

    def _last_op_idx(self, e):
        for i in range(len(self.ops[e]) - 1, -1, -1):
            if self.ops[e][i].kind in ("op", "dma"):
                return i
        return None

    def watermark(self, e):
        i = self._last_op_idx(e)
        if i is None:
            return None
        op = self.ops[e][i]
        if op.kind == "dma":
            return ("lane",) + op.lane
        return ("op", e, i)

    def _pre_edges(self, e, reads, writes):
        for b in reads:
            self._edge(e, b.writer)
            if b.is_psum:
                for key, v in list(b.readers.items()):
                    if key != e and not isinstance(key, tuple):
                        self._edge(e, ("op", key, v))
        for b in writes:
            self._edge(e, b.writer)
            for r in self._reader_refs(b):
                self._edge(e, r)

    @staticmethod
    def _add_reader(b, ref):
        if ref[0] == "op":
            _, pe, pidx = ref
            b.readers[pe] = max(b.readers.get(pe, -1), pidx)
        else:
            _, lane, v = ref
            key = ("lane", lane)
            b.readers[key] = max(b.readers.get(key, 0), v)

    def _reader_refs(self, b):
        out = []
        for key, v in b.readers.items():
            if isinstance(key, tuple):
                out.append(("lane", key[1], v))
            else:
                out.append(("op", key, v))
        return out

    def _post_update(self, e, ref, reads, writes):
        for b in reads:
            self._add_reader(b, ref)
        for b in writes:
            b.readers = {}
            b.writer = ref

    def emit(self, e, fn, reads=(), writes=(), accum=None):
        if accum is not None:
            for r in self._reader_refs(accum):
                self._edge(e, r)
            accum.readers = {}
            self._pre_edges(e, reads, ())
        else:
            self._pre_edges(e, reads, writes)
        self.ops[e].append(_Op("op", fn))
        ref = ("op", e, len(self.ops[e]) - 1)
        if accum is not None:
            accum.writer = ref
            self._post_update(e, ref, reads, ())
        else:
            self._post_update(e, ref, reads, writes)
        return ref

    def emit_dma(self, q, fn, reads=(), writes=(), group=False, lane_buf=None):
        assert len(writes) == 1
        b = writes[0]
        lb = lane_buf if lane_buf is not None else b
        role = "sw" if q == "gpsimd" else "hw"
        if role not in lb.lanes:
            lb.lanes[role] = self.new_lane(lb.name[:10])
            self.lane_val[lb.lanes[role]] = 0
        lane = lb.lanes[role]
        for rb in reads:
            self._edge(q, rb.writer)
        if group:
            pass
        else:
            self._edge(q, b.writer)
            for r in self._reader_refs(b):
                self._edge(q, r)
        self.lane_val[lane] += 16
        v = self.lane_val[lane]
        op = _Op("dma", fn, lane=(lane, v))
        self.ops[q].append(op)
        ref = ("lane", lane, v)
        for rb in reads:
            self._add_reader(rb, ref)
        b.readers = {}
        b.writer = ref
        return ref

    def claim(self, e, b):
        self._edge(e, b.writer)
        for r in self._reader_refs(b):
            self._edge(e, r)
        b.writer = None
        b.readers = {}

    def barrier(self, engines=ENGINES):
        marks = {e: self.watermark(e) for e in engines}
        for e in engines:
            for pe, ref in marks.items():
                if pe != e:
                    self._edge(e, ref)
            for lane, v in self.lane_val.items():
                if v > 0:
                    self._edge(e, ("lane", lane, v))

    def wait_ref(self, e, ref):
        self._edge(e, ref)

    def finalize(self, block):
        for e in ENGINES:
            c = 0
            for op in self.ops[e]:
                if op.kind == "op" and op.need_inc:
                    c += 1
                    op.value = c

        for e in ENGINES:
            stream = self.ops[e]
            if not stream:
                continue

            def make_body(stream=stream, e=e):
                def body(eng):
                    watermark = defaultdict(int)
                    for op in stream:
                        if op.kind == "wait":
                            ref = op.payload
                            if ref[0] == "op":
                                _, pe, pidx = ref
                                v = self.ops[pe][pidx].value
                                assert v > 0, f"wait on {pe}[{pidx}] without inc"
                                sem = pe
                            else:
                                _, sem, v = ref
                            if watermark[sem] >= v:
                                continue
                            eng.wait_ge(self.sems[sem], v)
                            watermark[sem] = v
                        elif op.kind == "op":
                            instr = op.payload(eng)
                            if op.need_inc:
                                instr.then_inc(self.sems[e], 1)
                        elif op.kind == "dma":
                            op.payload(eng).then_inc(self.sems[op.lane[0]], 16)
                return body

            getattr(block, e)(make_body())

    def stats(self):
        out = {}
        for e in ENGINES:
            out[e] = dict(
                ops=sum(1 for o in self.ops[e] if o.kind == "op"),
                dmas=sum(1 for o in self.ops[e] if o.kind == "dma"),
                waits=sum(1 for o in self.ops[e] if o.kind == "wait"),
                incs=sum(1 for o in self.ops[e] if o.kind == "op" and o.need_inc),
            )
        out["lanes"] = self._lane_n
        return out



PARAM_LAYOUT = [
    ("pw_b", 8), ("dw_b", 8),
    ("ln1_g", 8), ("ln1_b", 8),
    ("ln2_g", 8), ("ln2_b", 8),
    ("ln3_g", 8), ("ln3_b", 8),
    ("mlp_b2", 8), ("fb2", 8),
    ("fln1_g", 8), ("fln1_b", 8),
    ("f_b", 24),
    ("mlp_b1", 32), ("fb1", 32),
    ("fln2_g", 32), ("fln2_b", 32),
    ("dwtaps", 32),
]

PROW_LAYOUT = [
    ("f_b", 2 * EXP), ("pw_b", D), ("mlp_b1", HID), ("mlp_b2", D),
    ("fb1", HID), ("fb2", D), ("dw_b", D), ("ln1_b", D), ("dwtaps", 4 * D),
]
PROW_OFF = {}
_poff = 0
for _nm, _nc2 in PROW_LAYOUT:
    PROW_OFF[_nm] = _poff
    _poff += _nc2
PROW_COLS = _poff
PARAM_OFF = {}
_off = 0
for _nm, _nc_ in PARAM_LAYOUT:
    PARAM_OFF[_nm] = _off
    _off += _nc_
PARAM_COLS = _off


def _cols(v):
    return np.ascontiguousarray(np.asarray(v, np.float32).reshape(-1, 128).T)


def _tiles(w_T):
    w_T = np.asarray(w_T, np.float32)
    iin, out = w_T.shape
    return np.ascontiguousarray(w_T.reshape(iin // 128, 128, out).transpose(1, 0, 2))



def build_program(phases=("A1", "A2", "B", "C1", "C2"), debug_out=None):
    nc = bass.Bass("TRN2", target_bir_lowering=False)

    xd = nc.dram_tensor("x", [128, 8, T], F32, kind="ExternalInput").ap()
    smaskd = nc.dram_tensor("smask", [128, 2], F32, kind="ExternalInput").ap()
    pcolsd = nc.dram_tensor("pcols", [128, PARAM_COLS], F32, kind="ExternalInput").ap()
    prowsd = nc.dram_tensor("prows", [1, PROW_COLS], F32, kind="ExternalInput").ap()
    identd = nc.dram_tensor("ident", [128, 128], F32, kind="ExternalInput").ap()
    onesrowd = nc.dram_tensor("onesrow", [1, 128], F32, kind="ExternalInput").ap()
    wpwd = nc.dram_tensor("wpw", [128, 8, D], F32, kind="ExternalInput").ap()
    wfwd = nc.dram_tensor("wfw", [128, 8, 2 * EXP], F32, kind="ExternalInput").ap()
    wdpd = nc.dram_tensor("wdp", [128, 12, D], F32, kind="ExternalInput").ap()
    wm1d = nc.dram_tensor("wm1", [128, 8, HID], F32, kind="ExternalInput").ap()
    wm2d = nc.dram_tensor("wm2", [128, 32, D], F32, kind="ExternalInput").ap()
    ww1d = nc.dram_tensor("ww1", [128, 8, HID], F32, kind="ExternalInput").ap()
    ww2d = nc.dram_tensor("ww2", [128, 32, D], F32, kind="ExternalInput").ap()
    yd = nc.dram_tensor("y", [128, 8, NREAL], F32, kind="ExternalOutput").ap()

    def scr(name, shape, dt=F32):
        kind = "ExternalOutput" if debug_out == name else "Internal"
        return nc.dram_tensor(name, shape, dt, kind=kind).ap()

    xb_dram = scr("xb_scr", [128, 8, T])
    xc_dram = scr("xc_scr", [128, 8, T])
    xm_dram = scr("xm_scr", [128, 8, NREAL])
    hn_dram = scr("hn_scr", [128, 32, NREAL], BF16)

    x_in = Buf(xd, "x_in")
    xb_d = Buf(xb_dram, "xb_d")
    xc_d = Buf(xc_dram, "xc_d")
    xm_d = Buf(xm_dram, "xm_d")
    hn_d = Buf(hn_dram, "hn_d")
    y_out = Buf(yd, "y_out")

    p = Prog(nc)
    stack = ExitStack()

    def sb(name, shape, dt=F32, st=None):
        h = (st or stack).enter_context(nc.sbuf_tensor(name, shape, dt))
        return Buf(h.ap(), name)

    def ps(name, banks=1, st=None):
        h = (st or stack).enter_context(nc.psum_tensor(name, [128, 512 * banks], F32))
        b = Buf(h.ap(), name)
        b.is_psum = True
        return b

    def r3(ap, k=8):
        return ap.rearrange("p (k w) -> p k w", k=k)

    pc = sb("pcols_sb", [128, PARAM_COLS])
    onesrow = sb("onesrow_sb", [1, 128])
    onesw_bf = sb("onesw_bf", [1, 512], BF16)
    smask = sb("smask_sb", [128, 2])
    ones_bf = sb("ones_bf", [128, 1], BF16)
    carry = sb("carry", [128, 12])
    epsrow = sb("epsrow", [1, 1])
    rowarena = sb("rowarena", [1, HID + D], BF16)

    def col(nm, i=0, n=1):
        o = PARAM_OFF[nm] + i
        return pc.ap[:, o:o + n]

    def load_row(name, tag, st, bf=False):
        o = PROW_OFF[name]
        n = dict(PROW_LAYOUT)[name]
        rb = sb(f"prw_{tag}", [1, n], F32, st=st)
        p.emit_dma("sync", lambda q, rb=rb, o=o, n=n: q.dma_start(
            out=rb.ap, in_=prowsd[0:1, o:o + n]), writes=[rb])
        if not bf:
            return rb
        rbb = sb(f"prwb_{tag}", [1, n], BF16, st=st)
        p.emit("vector", lambda e, rb=rb, rbb=rbb: e.tensor_copy(rbb.ap, rb.ap),
               reads=[rb], writes=[rbb])
        return rb, rbb

    def mk_rows(tag):
        rows = {}
        rows["negm"] = sb(f"row_negm_{tag}", [1, WA])
        rows["w2"] = sb(f"row_w2_{tag}", [1, 2 * WA])
        rows["sdp"] = sb(f"row_sd_{tag}", [1, 2 * WA])
        rows["invp"] = sb(f"row_inv_{tag}", [1, 2 * WA])
        rows["var"] = rows["sdp"]
        return rows

    rows2 = [mk_rows("a"), mk_rows("b")]
    rq = {}
    for nm in ["tot", "m1", "thr"]:
        rq[nm] = sb("row_" + nm, [1, 64])


    def ln_stats(rows, xbuf, width, nk, xbf, sqbf, pstat, composed=False,
                 pstat2=None, off2=None):
        if pstat2 is None:
            pstat2, off2, fresh2 = pstat, width, False
        else:
            fresh2 = True
        p.emit("vector", lambda e, xbuf=xbuf, xbf=xbf, n=nk * width: e.tensor_copy(
            xbf.ap[:, 0:n], xbuf.ap[:, 0:n]), reads=[xbuf], writes=[xbf])
        p.emit("scalar", lambda e, xbuf=xbuf, sqbf=sqbf, n=nk * width: e.activation(
            sqbf.ap[:, 0:n], xbuf.ap[:, 0:n], AF.Square), reads=[xbuf], writes=[sqbf])
        for k in range(nk):
            p.emit("tensor", lambda e, k=k, xbf=xbf, pstat=pstat, width=width, nk=nk: e.matmul(
                pstat.ap[0:1, 0:width], ones_bf.ap, xbf.ap[:, k * width:(k + 1) * width],
                start=(k == 0), stop=(k == nk - 1), skip_group_check=True),
                reads=[ones_bf, xbf], writes=[pstat] if k == 0 else (),
                accum=None if k == 0 else pstat)
        for k in range(nk):
            p.emit("tensor", lambda e, k=k, sqbf=sqbf, pstat2=pstat2, off2=off2,
                   width=width, nk=nk, fresh2=fresh2: e.matmul(
                pstat2.ap[0:1, off2:off2 + width], ones_bf.ap,
                sqbf.ap[:, k * width:(k + 1) * width],
                start=(k == 0 and fresh2), stop=(k == nk - 1), skip_group_check=True),
                reads=[ones_bf, sqbf], writes=() if pstat2 is pstat else ([pstat2] if k == 0 else ()),
                accum=pstat2 if (pstat2 is pstat or k > 0) else None)
        row_chain(rows, pstat, pstat2, off2, width, nk * 128, composed)

    def row_chain(rows, pstat, pstat2, off2, W, F, composed):
        p.emit("vector", lambda e, pstat=pstat, F=F, W=W, rows=rows: e.tensor_single_scalar(
            rows["negm"].ap[:, 0:W], pstat.ap[0:1, 0:W], -1.0 / F, ALU.mult),
            reads=[pstat], writes=[rows["negm"]])
        p.emit("vector", lambda e, W=W, rows=rows: e.tensor_tensor(
            rows["w2"].ap[:, 0:W], rows["negm"].ap[:, 0:W], rows["negm"].ap[:, 0:W],
            ALU.mult),
            reads=[rows["negm"]], writes=[rows["w2"]])
        p.emit("vector", lambda e, pstat2=pstat2, off2=off2, F=F, W=W, rows=rows:
               e.scalar_tensor_tensor(
            rows["var"].ap[:, 0:W], pstat2.ap[0:1, off2:off2 + W], 1.0 / F,
            rows["w2"].ap[:, 0:W], ALU.mult, ALU.subtract),
            reads=[pstat2, rows["w2"]], writes=[rows["var"]])
        p.emit("vector", lambda e, W=W, rows=rows: e.tensor_single_scalar(
            rows["var"].ap[:, 0:W], rows["var"].ap[:, 0:W], 0.0, ALU.max),
            reads=[rows["var"]], writes=[rows["var"]])
        p.emit("vector", lambda e, W=W, rows=rows: e.tensor_single_scalar(
            rows["w2"].ap[:, 0:W], rows["var"].ap[:, 0:W], float(EPS), ALU.add),
            reads=[rows["var"]], writes=[rows["w2"]])
        R = 2 * W if composed else W
        if composed:
            p.emit("vector", lambda e, W=W, rows=rows: e.tensor_scalar(
                rows["w2"].ap[:, W:2 * W], rows["var"].ap[:, 0:W],
                1.0 + float(EPS), float(EPS) * float(EPS), ALU.mult, ALU.add),
                reads=[rows["var"]], writes=[rows["w2"]])
        p.emit("scalar", lambda e, R=R, rows=rows: e.activation(
            rows["sdp"].ap[:, 0:R], rows["w2"].ap[:, 0:R], AF.Sqrt),
            reads=[rows["w2"]], writes=[rows["sdp"]])
        p.emit("vector", lambda e, R=R, rows=rows: e.reciprocal(
            rows["invp"].ap[:, 0:R], rows["sdp"].ap[:, 0:R]),
            reads=[rows["sdp"]], writes=[rows["invp"]])

    def bcast(rows, row, width, pbank, off=0, fresh=True):
        if row == "negm":
            buf, src = rows["negm"], rows["negm"].ap[:, 0:width]
        elif row == "inv":
            buf, src = rows["invp"], rows["invp"].ap[:, 0:width]
        else:
            buf, src = rows["invp"], rows["invp"].ap[:, width:2 * width]
        p.emit("tensor", lambda e, width=width, pbank=pbank, off=off,
               fresh=fresh, buf=buf, src=src: e.matmul(
            pbank.ap[:, off:off + width], onesrow.ap, src,
            start=fresh, stop=True, skip_group_check=True),
            reads=[onesrow, buf],
            writes=[pbank] if fresh else (), accum=None if fresh else pbank)

    def apply2(src, dst, width, nk, pneg, noff, pinv, ioff):
        p.emit("vector", lambda e, src=src, dst=dst, width=width, nk=nk, pneg=pneg,
               noff=noff: e.tensor_tensor(
            r3(dst.ap[:, 0:nk * width], nk), r3(src.ap[:, 0:nk * width], nk),
            pneg.ap[:, noff:noff + width].rearrange("p (o w) -> p o w", o=1).broadcast_to(
                (128, nk, width)), ALU.add),
            reads=[src, pneg], writes=[dst])
        p.emit("vector", lambda e, dst=dst, width=width, nk=nk, pinv=pinv,
               ioff=ioff: e.tensor_tensor(
            r3(dst.ap[:, 0:nk * width], nk), r3(dst.ap[:, 0:nk * width], nk),
            pinv.ap[:, ioff:ioff + width].rearrange("p (o w) -> p o w", o=1).broadcast_to(
                (128, nk, width)), ALU.mult),
            reads=[dst, pinv], writes=[dst])

    def pair_view(pbuf, w):
        return pbuf.ap.rearrange("p (b c) -> p b c", b=2)[:, :, 0:w]

    p.emit_dma("sync", lambda q: q.dma_start(out=pc.ap, in_=pcolsd), writes=[pc])
    p.emit_dma("sync", lambda q: q.dma_start(out=onesrow.ap, in_=onesrowd), writes=[onesrow])
    p.emit_dma("sync", lambda q: q.dma_start(out=smask.ap, in_=smaskd), writes=[smask])
    p.emit("vector", lambda e: e.memset(ones_bf.ap, 1.0), writes=[ones_bf])
    p.emit("vector", lambda e: e.memset(onesw_bf.ap, 1.0), writes=[onesw_bf])
    p.emit("vector", lambda e: e.memset(epsrow.ap, float(EPS)), writes=[epsrow])
    p.emit("vector", lambda e: e.memset(carry.ap, 0.5), writes=[carry])

    stackA = ExitStack()
    wfw = sb("wfw_bf", [128, 8 * 2 * EXP], BF16, st=stackA)
    wdp = sb("wdp_bf", [128, 12 * D], BF16, st=stackA)
    fbeff_row = sb("fbeff_row", [1, 2 * EXP], BF16, st=stackA)
    stackA1 = ExitStack()
    diag = sb("diag", [128, 32 * 128], BF16, st=stackA1)
    wpw = sb("wpw_bf", [128, 8 * D], BF16, st=stackA1)
    dwbeff_row = sb("dwbeff_row", [1, D], BF16, st=stackA1)
    pwb_row = sb("pwb_row", [1, D], BF16, st=stackA1)

    def load_weights(dst, dram_ap, nk, ncols, stages, scale_col=None, queue="scalar"):
        p.claim("vector", dst)
        for k in range(nk):
            stg = stages[k % 2]
            p.emit_dma(queue, lambda q, k=k, stg=stg, dram_ap=dram_ap, ncols=ncols:
                       q.dma_start(out=stg.ap[:, 0:ncols], in_=dram_ap[:, k, :]),
                       writes=[stg])
            if scale_col is None:
                p.emit("vector", lambda e, k=k, dst=dst, stg=stg, ncols=ncols:
                       e.tensor_copy(dst.ap[:, k * ncols:(k + 1) * ncols],
                                     stg.ap[:, 0:ncols]),
                       reads=[stg], writes=[dst] if k == nk - 1 else ())
            else:
                p.emit("vector", lambda e, k=k, dst=dst, stg=stg, ncols=ncols,
                       sc=scale_col(k): e.tensor_scalar(
                           dst.ap[:, k * ncols:(k + 1) * ncols], stg.ap[:, 0:ncols],
                           sc, None, ALU.mult),
                       reads=[stg, pc], writes=[dst] if k == nk - 1 else ())

    def prologue_A():
        with ExitStack() as st0:
            ident = sb("ident_sb", [128, 128], st=st0)
            dwg = sb("dwg", [128, 32], st=st0)
            drow = sb("drow", [1, D], st=st0)
            stages = [sb(f"wstgA{i}", [128, 4096], F32, st=st0) for i in range(2)]

            p.emit_dma("sync", lambda q, ident=ident: q.dma_start(out=ident.ap, in_=identd),
                       writes=[ident])
            p.claim("vector", dwg)
            for s in range(8):
                p.emit("vector", lambda e, s=s, dwg=dwg: e.tensor_scalar(
                    dwg.ap[:, s * 4:(s + 1) * 4], col("dwtaps", s * 4, 4), col("ln1_g", s),
                    None, ALU.mult),
                    reads=[pc], writes=[dwg] if s == 7 else ())
            p.claim("vector", diag)
            for i in range(32):
                p.emit("vector", lambda e, i=i, ident=ident, dwg=dwg: e.tensor_scalar(
                    diag.ap[:, i * 128:(i + 1) * 128], ident.ap, dwg.ap[:, i:i + 1],
                    None, ALU.mult),
                    reads=[ident, dwg], writes=[diag] if i == 31 else ())

            st0rows = stages[0]
            st1rows = stages[1]
            p.emit_dma("sync", lambda q, st1rows=st1rows: q.dma_start(
                out=st1rows.ap[0:1, 0:4 * D],
                in_=prowsd[0:1, PROW_OFF["dwtaps"]:PROW_OFF["dwtaps"] + 4 * D]),
                writes=[st1rows])
            p.emit_dma("sync", lambda q, st0rows=st0rows: q.dma_start(
                out=st0rows.ap[0:1, 0:D],
                in_=prowsd[0:1, PROW_OFF["dw_b"]:PROW_OFF["dw_b"] + D]),
                writes=[st0rows])
            p.emit_dma("sync", lambda q, st0rows=st0rows: q.dma_start(
                out=st0rows.ap[0:1, D:2 * D],
                in_=prowsd[0:1, PROW_OFF["ln1_b"]:PROW_OFF["ln1_b"] + D]),
                writes=[st0rows], group=True)
            p.emit_dma("sync", lambda q, st0rows=st0rows: q.dma_start(
                out=st0rows.ap[0:1, 2 * D:3 * D],
                in_=prowsd[0:1, PROW_OFF["pw_b"]:PROW_OFF["pw_b"] + D]),
                writes=[st0rows], group=True)
            p.emit("vector", lambda e, st0rows=st0rows: e.tensor_copy(
                pwb_row.ap, st0rows.ap[0:1, 2 * D:3 * D]),
                reads=[st0rows], writes=[pwb_row])
            p.emit("vector", lambda e, drow=drow, st1rows=st1rows: e.tensor_reduce(
                drow.ap[:, 0:D].rearrange("o (c u) -> o c u", u=1),
                st1rows.ap[0:1, 0:4 * D].rearrange("o (c k) -> o c k", k=4),
                mybir.AxisListType.X, ALU.add), reads=[st1rows], writes=[drow])
            p.emit("vector", lambda e, drow=drow, st0rows=st0rows: e.tensor_tensor(
                drow.ap[:, 0:D], drow.ap[:, 0:D], st0rows.ap[0:1, D:2 * D], ALU.mult),
                reads=[drow, st0rows], writes=[drow])
            p.emit("vector", lambda e, drow=drow, st0rows=st0rows: e.tensor_tensor(
                drow.ap[:, 0:D], drow.ap[:, 0:D], st0rows.ap[0:1, 0:D], ALU.add),
                reads=[drow, st0rows], writes=[drow])
            p.emit("vector", lambda e, drow=drow: e.tensor_copy(
                dwbeff_row.ap, drow.ap[:, 0:D]), reads=[drow], writes=[dwbeff_row])

            load_weights(wpw, wpwd, 8, D, stages)
            load_weights(wfw, wfwd, 8, 2 * EXP, stages, scale_col=lambda k: col("ln2_g", k))
            load_weights(wdp, wdpd, 12, D, stages)
            p.barrier()

    prologue_A()

    def phase_A1():
        NT = T // WA
        with ExitStack() as stA:
            xt = [sb(f"xtA1_{i}", [128, 8 * WA], st=stA) for i in range(3)]
            tw = sb("twA1", [128, 8 * WA], st=stA)
            h1bf = [sb(f"h1bfA1_{i}", [128, 8 * (WA + 3)], BF16, st=stA) for i in range(2)]
            xbf = sb("xbfA1", [128, 8 * WA], BF16, st=stA)
            sqbf = sb("sqbfA1", [128, 8 * WA], BF16, st=stA)
            cbf = sb("cbfA1", [128, 8 * WA], BF16, st=stA)
            ps1 = ps("psA1_s1", st=stA)
            ps2 = ps("psA1_s2", st=stA)
            pb1 = ps("psA1_b1", st=stA)
            pb2 = ps("psA1_b2", st=stA)
            pmc = ps("psA1_conv", banks=2, st=stA)
            pmw = ps("psA1_pw", banks=2, st=stA)

            p.emit("vector", lambda e: e.memset(
                h1bf[1].ap.rearrange("p (s c) -> p s c", s=8)[:, :, WA:WA + 3], 0.0),
                writes=[h1bf[1]])

            def front_a(j):
                xcur = xt[j % 3]
                rows = rows2[j % 2]
                t0 = j * WA
                p.emit_dma("sync", lambda q, t0=t0, xcur=xcur: q.dma_start(
                    out=r3(xcur.ap), in_=xd[:, :, t0:t0 + WA]), writes=[xcur])
                ln_stats(rows, xcur, WA, 8, xbf, sqbf, ps1, composed=True,
                         pstat2=ps2, off2=0)

            def front_b(j):
                xcur = xt[j % 3]
                hcur = h1bf[j % 2]
                hprev = h1bf[(j + 1) % 2]
                rows = rows2[j % 2]
                bcast(rows, "negm", WA, pb1)
                bcast(rows, "inv", WA, pb2)
                p.emit("vector", lambda e, xcur=xcur: e.tensor_tensor(
                    r3(tw.ap), r3(xcur.ap),
                    pb1.ap[:, 0:WA].rearrange("p (o w) -> p o w", o=1).broadcast_to(
                        (128, 8, WA)), ALU.add), reads=[xcur, pb1], writes=[tw])
                p.emit("vector", lambda e, xcur=xcur: e.tensor_tensor(
                    r3(xcur.ap), r3(tw.ap),
                    pb2.ap[:, 0:WA].rearrange("p (o w) -> p o w", o=1).broadcast_to(
                        (128, 8, WA)), ALU.mult), reads=[tw, pb2], writes=[xcur])
                bcast(rows, "inv12", WA, pb1)
                p.emit("vector", lambda e: e.tensor_tensor(
                    r3(tw.ap), r3(tw.ap),
                    pb1.ap[:, 0:WA].rearrange("p (o w) -> p o w", o=1).broadcast_to(
                        (128, 8, WA)), ALU.mult), reads=[tw, pb1], writes=[tw])
                p.emit("scalar", lambda e, hcur=hcur: e.activation(
                    hcur.ap.rearrange("p (s c) -> p s c", s=8)[:, :, 3:WA + 3], r3(tw.ap),
                    AF.Copy), reads=[tw], writes=[hcur])
                p.emit("gpsimd", lambda e, hcur=hcur, hprev=hprev: e.tensor_copy(
                    hcur.ap.rearrange("p (s c) -> p s c", s=8)[:, :, 0:3],
                    hprev.ap.rearrange("p (s c) -> p s c", s=8)[:, :, WA:WA + 3]),
                    reads=[hprev], writes=[hcur])

            def back1(j):
                hcur = h1bf[j % 2]
                p.claim("scalar", cbf)
                for sp in range(4):
                    for half in range(2):
                        s = sp * 2 + half
                        po = half * 512
                        p.emit("tensor", lambda e, s=s, po=po: e.matmul(
                            pmc.ap[:, po:po + WA], dwbeff_row.ap[:, s * 128:(s + 1) * 128],
                            onesw_bf.ap[:, 0:WA], start=True, stop=False,
                            skip_group_check=True),
                            reads=[dwbeff_row, onesw_bf],
                            writes=[pmc] if half == 0 else (),
                            accum=None if half == 0 else pmc)
                        for k in range(4):
                            p.emit("tensor", lambda e, s=s, k=k, po=po, hcur=hcur: e.matmul(
                                pmc.ap[:, po:po + WA],
                                diag.ap[:, (s * 4 + k) * 128:(s * 4 + k + 1) * 128],
                                hcur.ap[:, s * (WA + 3) + k: s * (WA + 3) + k + WA],
                                start=False, stop=(k == 3), skip_group_check=True),
                                reads=[diag, hcur], writes=(), accum=pmc)
                    p.emit("scalar", lambda e, sp=sp: e.activation(
                        cbf.ap[:, sp * 2 * WA:(sp + 1) * 2 * WA].rearrange(
                            "p (b w) -> p b w", b=2), pair_view(pmc, WA), AF.Copy),
                        reads=[pmc], writes=[cbf] if sp == 3 else ())

            def back2(j):
                xcur = xt[j % 3]
                t0 = j * WA
                for op_ in range(4):
                    for half in range(2):
                        o = op_ * 2 + half
                        po = half * 512
                        p.emit("tensor", lambda e, o=o, po=po: e.matmul(
                            pmw.ap[:, po:po + WA], pwb_row.ap[:, o * 128:(o + 1) * 128],
                            onesw_bf.ap[:, 0:WA], start=True, stop=False,
                            skip_group_check=True),
                            reads=[pwb_row, onesw_bf],
                            writes=[pmw] if half == 0 else (),
                            accum=None if half == 0 else pmw)
                        for k in range(8):
                            p.emit("tensor", lambda e, o=o, k=k, po=po: e.matmul(
                                pmw.ap[:, po:po + WA],
                                wpw.ap[:, k * D + o * 128: k * D + o * 128 + 128],
                                cbf.ap[:, k * WA:(k + 1) * WA],
                                start=False, stop=(k == 7), skip_group_check=True),
                                reads=[wpw, cbf], writes=(), accum=pmw)
                    p.emit("vector", lambda e, op_=op_, xcur=xcur: e.tensor_tensor(
                        r3(xcur.ap[:, op_ * 2 * WA:(op_ + 1) * 2 * WA], 2),
                        pair_view(pmw, WA),
                        r3(xcur.ap[:, op_ * 2 * WA:(op_ + 1) * 2 * WA], 2), ALU.add),
                        reads=[pmw, xcur], writes=[xcur])

                p.emit_dma("gpsimd", lambda q, t0=t0, xcur=xcur: q.dma_start(
                    out=xb_dram[:, :, t0:t0 + WA], in_=r3(xcur.ap)),
                    reads=[xcur], writes=[xb_d], group=True, lane_buf=xcur)

            for j in range(NT):
                front_a(j)
                if j >= 1:
                    back1(j - 1)
                front_b(j)
                if j >= 1:
                    back2(j - 1)
            back1(NT - 1)
            back2(NT - 1)
            p.barrier()

    if "A1" in phases:
        phase_A1()
    stackA1.close()

    def phase_A2():
        NT = T // WM
        with ExitStack() as stF:
            gi = sb("gi2col", [128, 16], F32, st=stF)
            fb_r = load_row("f_b", "fbA2", stF)
            gib = sb("gib2", [128, 8], BF16, st=stF)
            frow = sb("frow", [1, 2 * EXP], F32, st=stF)
            pq = ps("psA2_f", st=stF)
            p.emit("vector", lambda e, gi=gi: e.reciprocal(
                gi.ap[:, 0:8], col("ln2_g", 0, 8)), reads=[pc], writes=[gi])
            p.emit("vector", lambda e, gi=gi: e.tensor_tensor(
                gi.ap[:, 8:16], col("ln2_b", 0, 8), gi.ap[:, 0:8], ALU.mult),
                reads=[pc, gi], writes=[gi])
            p.emit("vector", lambda e, gi=gi, gib=gib: e.tensor_copy(
                gib.ap, gi.ap[:, 8:16]), reads=[gi], writes=[gib])
            p.claim("vector", frow)
            for cc in range(6):
                for k in range(8):
                    p.emit("tensor", lambda e, cc=cc, k=k, gib=gib, pq=pq: e.matmul(
                        pq.ap[0:1, 0:512], gib.ap[:, k:k + 1],
                        wfw.ap[:, k * 2 * EXP + cc * 512: k * 2 * EXP + cc * 512 + 512],
                        start=(k == 0), stop=(k == 7)),
                        reads=[gib, wfw], writes=[pq] if k == 0 else (),
                        accum=None if k == 0 else pq)
                p.emit("vector", lambda e, cc=cc, frow=frow, pq=pq, fb_r=fb_r:
                       e.scalar_tensor_tensor(
                    frow.ap[:, cc * 512:(cc + 1) * 512], pq.ap[0:1, 0:512], 1.0,
                    fb_r.ap[:, cc * 512:(cc + 1) * 512], ALU.mult, ALU.add),
                    reads=[pq, fb_r], writes=[frow] if cc == 5 else ())
            p.emit("vector", lambda e, frow=frow: e.tensor_copy(fbeff_row.ap, frow.ap),
                   reads=[frow], writes=[fbeff_row])
            p.barrier()

        with ExitStack() as stA:
            xbt = [sb(f"xbtA2_{i}", [128, 8 * WM], st=stA) for i in range(3)]
            twa = sb("twA2", [128, 8 * WM], st=stA)
            xbf2 = sb("xbfA2", [128, 8 * WM], BF16, st=stA)
            sqbf2 = sb("sqbfA2", [128, 8 * WM], BF16, st=stA)
            gbf = [sb(f"gbfA2_{i}", [128, 8 * WM], BF16, st=stA) for i in range(2)]
            h_raw = sb("h_rawA2", [128, 2 * WM], st=stA)
            a_s = sb("a_sA2", [128, 2 * WM], st=stA)
            z_s = sb("z_sA2", [128, 2 * WM], st=stA)
            gt_s = sb("gt_sA2", [128, 2 * WM], st=stA)
            v_s = sb("v_sA2", [128, 2 * WM], st=stA)
            hseq = sb("hseqA2", [128, 12 * WM], st=stA)
            hbf = sb("hbfA2", [128, 12 * WM], BF16, st=stA)
            xct = [sb("xctA2_0", [128, 8 * WM], st=stA)] * 2
            psb = ps("psA2_sb", st=stA)
            pbb = ps("psA2_bb", st=stA)
            pmKs = [ps(f"psA2_K{i}", banks=2, st=stA) for i in range(2)]
            pmH = ps("psA2_H", banks=2, st=stA)

            def front_a(j):
                xcur = xbt[j % 3]
                rows = rows2[j % 2]
                t0 = j * WM
                p.emit_dma("sync", lambda q, t0=t0, xcur=xcur: q.dma_start(
                    out=r3(xcur.ap), in_=xb_dram[:, :, t0:t0 + WM]),
                    reads=[xb_d], writes=[xcur])
                ln_stats(rows, xcur, WM, 8, xbf2, sqbf2, psb)

            def front_b(j):
                xcur = xbt[j % 3]
                rows = rows2[j % 2]
                gcur = gbf[j % 2]
                bcast(rows, "negm", WM, pbb, 0, True)
                bcast(rows, "inv", WM, pbb, WM, False)
                apply2(xcur, twa, WM, 8, pbb, 0, pbb, WM)
                p.emit("scalar", lambda e, gcur=gcur: e.activation(
                    gcur.ap, twa.ap, AF.Copy), reads=[twa], writes=[gcur])

            def back_q(j, cps):
                gcur = gbf[j % 2]
                warm_cols = min(WM, max(0, WARM - j * WM))
                for cp in cps:
                    c0 = cp * 2
                    pmK = pmKs[cp % 2]
                    for half in range(2):
                        c = c0 + half
                        po = half * 512
                        p.emit("tensor", lambda e, c=c, po=po, pmK=pmK: e.matmul(
                            pmK.ap[:, po:po + WM], fbeff_row.ap[:, c * 128:(c + 1) * 128],
                            onesw_bf.ap[:, 0:WM], start=True, stop=False, skip_group_check=True),
                            reads=[fbeff_row, onesw_bf],
                            writes=[pmK] if half == 0 else (),
                            accum=None if half == 0 else pmK)
                        for k in range(8):
                            p.emit("tensor", lambda e, c=c, k=k, po=po, pmK=pmK,
                                   gcur=gcur: e.matmul(
                                pmK.ap[:, po:po + WM],
                                wfw.ap[:, k * 2 * EXP + c * 128: k * 2 * EXP + c * 128 + 128],
                                gcur.ap[:, k * WM:(k + 1) * WM],
                                start=False, stop=(k == 7), skip_group_check=True),
                                reads=[wfw, gcur], writes=(), accum=pmK)
                    for half in range(2):
                        c = c0 + half
                        po = half * 512
                        p.emit("tensor", lambda e, c=c, po=po: e.matmul(
                            pmH.ap[:, po:po + WM],
                            fbeff_row.ap[:, (12 + c) * 128:(13 + c) * 128],
                            onesw_bf.ap[:, 0:WM], start=True, stop=False, skip_group_check=True),
                            reads=[fbeff_row, onesw_bf],
                            writes=[pmH] if half == 0 else (),
                            accum=None if half == 0 else pmH)
                        for k in range(8):
                            p.emit("tensor", lambda e, c=c, k=k, po=po, gcur=gcur: e.matmul(
                                pmH.ap[:, po:po + WM],
                                wfw.ap[:, k * 2 * EXP + (12 + c) * 128: k * 2 * EXP + (12 + c) * 128 + 128],
                                gcur.ap[:, k * WM:(k + 1) * WM],
                                start=False, stop=(k == 7), skip_group_check=True),
                                reads=[wfw, gcur], writes=(), accum=pmH)
                    p.emit("scalar", lambda e, pmK=pmK: e.activation(
                        r3(a_s.ap, 2), pair_view(pmK, WM), AF.Sigmoid, scale=-1.0),
                        reads=[pmK], writes=[a_s])
                    p.emit("scalar", lambda e, pmK=pmK: e.activation(
                        r3(z_s.ap, 2), pair_view(pmK, WM), AF.Sigmoid),
                        reads=[pmK], writes=[z_s])
                    p.emit("scalar", lambda e: e.activation(
                        r3(h_raw.ap, 2), pair_view(pmH, WM), AF.Copy),
                        reads=[pmH], writes=[h_raw])
                    p.emit("scalar", lambda e: e.activation(
                        gt_s.ap, h_raw.ap, AF.Sigmoid),
                        reads=[h_raw], writes=[gt_s])
                    p.emit("scalar", lambda e: e.activation(
                        v_s.ap, h_raw.ap, AF.Copy, bias=0.5),
                        reads=[h_raw], writes=[v_s])
                    p.emit("vector", lambda e: e.tensor_tensor(gt_s.ap, gt_s.ap, v_s.ap,
                                                               ALU.max),
                           reads=[v_s, gt_s], writes=[gt_s])
                    p.emit("vector", lambda e: e.tensor_tensor(v_s.ap, z_s.ap, gt_s.ap,
                                                               ALU.mult),
                           reads=[z_s, gt_s], writes=[v_s])
                    if warm_cols > 0:
                        wc = warm_cols
                        p.emit("vector", lambda e, wc=wc: e.tensor_scalar(
                            r3(a_s.ap, 2)[:, :, 0:wc], r3(a_s.ap, 2)[:, :, 0:wc],
                            smask.ap[:, 0:1], smask.ap[:, 1:2], ALU.mult, ALU.add),
                            reads=[a_s, smask], writes=[a_s])
                        p.emit("vector", lambda e, wc=wc: e.tensor_scalar(
                            r3(v_s.ap, 2)[:, :, 0:wc], r3(v_s.ap, 2)[:, :, 0:wc],
                            smask.ap[:, 0:1], None, ALU.mult),
                            reads=[v_s, smask], writes=[v_s])
                    for half in range(2):
                        c = c0 + half
                        p.emit("vector", lambda e, c=c, half=half: e.tensor_tensor_scan(
                            hseq.ap[:, c * WM:(c + 1) * WM],
                            a_s.ap[:, half * WM:(half + 1) * WM],
                            v_s.ap[:, half * WM:(half + 1) * WM],
                            carry.ap[:, c:c + 1], ALU.mult, ALU.add),
                            reads=[a_s, v_s, carry],
                            writes=[hseq] if c == 11 else ())
                    p.emit("vector", lambda e, c0=c0: e.tensor_copy(
                        hbf.ap[:, c0 * WM:(c0 + 2) * WM],
                        hseq.ap[:, c0 * WM:(c0 + 2) * WM]),
                        reads=[hseq], writes=[hbf])
            def back_dp(j):
                xcur = xbt[j % 3]
                xco = xct[j % 2]
                t0 = j * WM
                p.emit("vector", lambda e: e.tensor_copy(
                    carry.ap.rearrange("p (c o) -> p c o", o=1),
                    hseq.ap.rearrange("p (c w) -> p c w", c=12)[:, :, WM - 1:WM]),
                    reads=[hseq], writes=[carry])

                for k in range(12):
                    for o in range(8):
                        pmD = pmKs[o // 4]
                        po = ((o // 2) % 2) * 512 + (o % 2) * WM
                        p.emit("tensor", lambda e, o=o, k=k, po=po, pmD=pmD: e.matmul(
                            pmD.ap[:, po:po + WM],
                            wdp.ap[:, k * D + o * 128: k * D + o * 128 + 128],
                            hbf.ap[:, k * WM:(k + 1) * WM],
                            start=(k == 0 and o % 2 == 0), stop=(k == 11),
                            skip_group_check=True),
                            reads=[wdp, hbf],
                            writes=[pmD] if (k == 0 and o % 4 == 0) else (),
                            accum=None if (k == 0 and o % 4 == 0) else pmD)
                for op_ in range(4):
                    pmD = pmKs[op_ // 2]
                    po = (op_ % 2) * 512
                    p.emit("vector", lambda e, op_=op_, po=po, xcur=xcur, xco=xco,
                           pmD=pmD: e.tensor_tensor(
                        xco.ap[:, op_ * 2 * WM:(op_ + 1) * 2 * WM],
                        pmD.ap[:, po:po + 2 * WM],
                        xcur.ap[:, op_ * 2 * WM:(op_ + 1) * 2 * WM], ALU.add),
                        reads=[pmD, xcur], writes=[xco])
                p.emit_dma("gpsimd", lambda q, t0=t0, xco=xco: q.dma_start(
                    out=xc_dram[:, :, t0:t0 + WM], in_=r3(xco.ap)),
                    reads=[xco], writes=[xc_d], group=True, lane_buf=xco)

            def back1(j):
                back_q(j, range(0, 3))

            def back2(j):
                back_q(j, range(3, 6))
                back_dp(j)

            for j in range(NT):
                front_a(j)
                if j >= 1:
                    back1(j - 1)
                front_b(j)
                if j >= 1:
                    back2(j - 1)
            back1(NT - 1)
            back2(NT - 1)
            p.barrier()

    if "A2" in phases:
        phase_A2()
    p.barrier()
    stackA.close()

    def phase_B():
        NT = NREAL // WM
        with ExitStack() as stB:
            wm1 = sb("wm1_bf", [128, 8 * HID], BF16, st=stB)
            wm2 = sb("wm2_bf", [128, 32 * D], BF16, st=stB)
            b1eff_row = Buf(rowarena.ap[:, 0:HID], "b1eff_row")
            mb2_row = Buf(rowarena.ap[:, HID:HID + D], "mb2_row")
            psb = ps("psB_sb", st=stB)
            pbb = ps("psB_bb", st=stB)
            pmB = [ps(f"psB_m1{i}", st=stB) for i in range(2)]
            pmm2 = ps("psB_mm2", banks=4, st=stB)

            with ExitStack() as stB0:
                stages = [sb(f"wstgB{i}", [128, 2048], F32, st=stB0) for i in range(2)]
                stages2 = [sb(f"wstgB2{i}", [128, 1024], F32, st=stB0) for i in range(2)]
                gi = sb("gi3col", [128, 16], F32, st=stB0)
                mb1_r = load_row("mlp_b1", "mb1", stB0)
                mb2f = load_row("mlp_b2", "mb2", stB0)
                p.emit("vector", lambda e, mb2f=mb2f: e.tensor_copy(mb2_row.ap, mb2f.ap),
                       reads=[mb2f], writes=[mb2_row])
                gib = sb("gib3", [128, 8], BF16, st=stB0)
                p.claim("vector", wm1)
                for k in range(8):
                    for h2 in range(2):
                        stg = stages[(k * 2 + h2) % 2]
                        p.emit_dma("scalar", lambda q, k=k, h2=h2, stg=stg: q.dma_start(
                            out=stg.ap, in_=wm1d[:, k, h2 * 2048:(h2 + 1) * 2048]),
                            writes=[stg])
                        p.emit("vector", lambda e, k=k, h2=h2, stg=stg: e.tensor_scalar(
                            wm1.ap[:, k * HID + h2 * 2048: k * HID + (h2 + 1) * 2048],
                            stg.ap, col("ln3_g", k), None, ALU.mult),
                            reads=[stg, pc],
                            writes=[wm1] if (k == 7 and h2 == 1) else ())
                p.claim("vector", wm2)
                for k in range(32):
                    stg = stages2[k % 2]
                    p.emit_dma("sync", lambda q, k=k, stg=stg: q.dma_start(
                        out=stg.ap[:, 0:D], in_=wm2d[:, k, :]), writes=[stg])
                    p.emit("vector", lambda e, k=k, stg=stg: e.tensor_copy(
                        wm2.ap[:, k * D:(k + 1) * D], stg.ap[:, 0:D]),
                        reads=[stg], writes=[wm2] if k == 31 else ())
                p.emit("vector", lambda e, gi=gi: e.reciprocal(
                    gi.ap[:, 0:8], col("ln3_g", 0, 8)), reads=[pc], writes=[gi])
                p.emit("vector", lambda e, gi=gi: e.tensor_tensor(
                    gi.ap[:, 8:16], col("ln3_b", 0, 8), gi.ap[:, 0:8], ALU.mult),
                    reads=[pc, gi], writes=[gi])
                p.emit("vector", lambda e, gi=gi, gib=gib: e.tensor_copy(
                    gib.ap, gi.ap[:, 8:16]), reads=[gi], writes=[gib])
                p.claim("vector", b1eff_row)
                for cc in range(8):
                    for k in range(8):
                        p.emit("tensor", lambda e, cc=cc, k=k, gib=gib: e.matmul(
                            psb.ap[0:1, 0:512], gib.ap[:, k:k + 1],
                            wm1.ap[:, k * HID + cc * 512: k * HID + cc * 512 + 512],
                            start=(k == 0), stop=(k == 7)),
                            reads=[gib, wm1], writes=[psb] if k == 0 else (),
                            accum=None if k == 0 else psb)
                    p.emit("vector", lambda e, cc=cc, mb1_r=mb1_r:
                           e.scalar_tensor_tensor(
                        b1eff_row.ap[:, cc * 512:(cc + 1) * 512], psb.ap[0:1, 0:512], 1.0,
                        mb1_r.ap[:, cc * 512:(cc + 1) * 512], ALU.mult, ALU.add),
                        reads=[psb, mb1_r], writes=[b1eff_row] if cc == 7 else ())
                p.barrier()

            xctB = [sb(f"xctB_{i}", [128, 8 * WM], st=stB) for i in range(2)]
            twb = sb("twB", [128, 8 * WM], st=stB)
            sqbfB = sb("sqbfB", [128, 8 * WM], BF16, st=stB)
            g3bf = [sb(f"g3bfB_{i}", [128, 8 * WM], BF16, st=stB) for i in range(2)]
            hstrip = sb("hstripB", [128, 8 * WM], BF16, st=stB)
            xdB = sb("xdB", [128, 8 * WM], st=stB)

            def front_a(j):
                xcur = xctB[j % 2]
                rows = rows2[j % 2]
                gcur = g3bf[j % 2]
                t0 = WARM + j * WM
                p.emit_dma("sync", lambda q, t0=t0, xcur=xcur: q.dma_start(
                    out=r3(xcur.ap), in_=xc_dram[:, :, t0:t0 + WM]),
                    reads=[xc_d], writes=[xcur])
                ln_stats(rows, xcur, WM, 8, gcur, sqbfB, psb)

            def front_b(j):
                xcur = xctB[j % 2]
                rows = rows2[j % 2]
                gcur = g3bf[j % 2]
                bcast(rows, "negm", WM, pbb, 0, True)
                bcast(rows, "inv", WM, pbb, WM, False)
                apply2(xcur, twb, WM, 8, pbb, 0, pbb, WM)
                p.emit("scalar", lambda e, gcur=gcur: e.activation(
                    gcur.ap, twb.ap, AF.Copy), reads=[twb], writes=[gcur])

            def back_mm(j, hcs):
                gcur = g3bf[j % 2]
                for hc in hcs:
                    p.claim("scalar", hstrip)
                    for s_ in range(8):
                        hidx = hc * 8 + s_
                        bank = pmB[s_ % 2]
                        p.emit("tensor", lambda e, hidx=hidx, bank=bank: e.matmul(
                            bank.ap[:, 0:WM],
                            b1eff_row.ap[:, hidx * 128:(hidx + 1) * 128],
                            onesw_bf.ap[:, 0:WM], start=True, stop=False, skip_group_check=True),
                            reads=[b1eff_row, onesw_bf], writes=[bank])
                        for k in range(8):
                            p.emit("tensor", lambda e, hidx=hidx, k=k, bank=bank,
                                   gcur=gcur: e.matmul(
                                bank.ap[:, 0:WM],
                                wm1.ap[:, k * HID + hidx * 128: k * HID + hidx * 128 + 128],
                                gcur.ap[:, k * WM:(k + 1) * WM],
                                start=False, stop=(k == 7), skip_group_check=True),
                                reads=[wm1, gcur], writes=(), accum=bank)
                        p.emit("scalar", lambda e, s_=s_, bank=bank: e.activation(
                            hstrip.ap[:, s_ * WM:(s_ + 1) * WM], bank.ap[:, 0:WM],
                            AF.Gelu),
                            reads=[bank], writes=[hstrip] if s_ == 7 else ())
                    for o in range(8):
                        if hc == 0:
                            p.emit("tensor", lambda e, o=o: e.matmul(
                                pmm2.ap[:, o * WM:(o + 1) * WM],
                                mb2_row.ap[:, o * 128:(o + 1) * 128], onesw_bf.ap[:, 0:WM],
                                start=(o % 2 == 0), stop=False, skip_group_check=True),
                                reads=[mb2_row, onesw_bf],
                                writes=[pmm2] if o == 0 else (),
                                accum=None if o == 0 else pmm2)
                        for s in range(8):
                            hidx = hc * 8 + s
                            p.emit("tensor", lambda e, o=o, s=s, hidx=hidx: e.matmul(
                                pmm2.ap[:, o * WM:(o + 1) * WM],
                                wm2.ap[:, hidx * D + o * 128: hidx * D + o * 128 + 128],
                                hstrip.ap[:, s * WM:(s + 1) * WM],
                                start=False, stop=(hc == 3 and s == 7),
                                skip_group_check=True),
                                reads=[wm2, hstrip], writes=(), accum=pmm2)
            def back_ln(j):
                xcur = xctB[j % 2]
                rows = rows2[j % 2]
                gcur = g3bf[j % 2]
                t0 = WARM + j * WM
                p.claim("vector", xdB)
                for op_ in range(4):
                    p.emit("vector", lambda e, op_=op_, xcur=xcur: e.tensor_tensor(
                        xdB.ap[:, op_ * 2 * WM:(op_ + 1) * 2 * WM],
                        pmm2.ap[:, op_ * 2 * WM:(op_ + 1) * 2 * WM],
                        xcur.ap[:, op_ * 2 * WM:(op_ + 1) * 2 * WM], ALU.add),
                        reads=[pmm2, xcur], writes=[xdB] if op_ == 3 else ())

                ln_stats(rows, xdB, WM, 8, gcur, sqbfB, psb)
                bcast(rows, "negm", WM, pbb, 0, True)
                bcast(rows, "inv", WM, pbb, WM, False)
                apply2(xdB, twb, WM, 8, pbb, 0, pbb, WM)
                p.emit_dma("scalar", lambda q, t0=t0, xdB=xdB: q.dma_start(
                    out=r3(xdB.ap), in_=xd[:, :, t0:t0 + WM]),
                    reads=[x_in], writes=[xdB])
                p.emit("vector", lambda e, xdB=xdB: e.tensor_tensor(
                    xdB.ap, xdB.ap, twb.ap, ALU.add),
                    reads=[xdB, twb], writes=[xdB])
                p.emit_dma("gpsimd", lambda q, j=j, xdB=xdB: q.dma_start(
                    out=xm_dram[:, :, j * WM:(j + 1) * WM], in_=r3(xdB.ap)),
                    reads=[xdB], writes=[xm_d], group=True, lane_buf=xdB)

            def back1(j):
                back_mm(j, range(0, 2))

            def back2(j):
                back_mm(j, range(2, 4))
                back_ln(j)

            for j in range(NT):
                front_a(j)
                if j >= 1:
                    back1(j - 1)
                front_b(j)
                if j >= 1:
                    back2(j - 1)
            back1(NT - 1)
            back2(NT - 1)
            p.barrier()

    if "B" in phases:
        phase_B()

    def quantize(dram_ap, nk, ncols, dst_bf, g_col, st, tag):
        mcol = sb(f"wq_mcol_{tag}", [128, 1], F32, st=st)
        mrow = sb(f"wq_mrow_{tag}", [1, 1], F32, st=st)
        with ExitStack() as stq:
            stages = [sb(f"wq_stage_{tag}{i}", [128, ncols], F32, st=stq) for i in range(2)]
            q1 = sb(f"wq_q1_{tag}", [128, ncols], F32, st=stq)
            q2 = sb(f"wq_q2_{tag}", [128, ncols], F32, st=stq)
            asum = sb(f"wq_asum_{tag}", [128, nk], F32, st=stq)
            onecol = sb(f"wq_one_{tag}", [128, 1], F32, st=stq)
            thrcol = sb(f"wq_thr_{tag}", [128, 2], F32, st=stq)
            pq = ps(f"wq_ps_{tag}", st=stq)
            p.emit("vector", lambda e, onecol=onecol: e.memset(onecol.ap, 1.0),
                   writes=[onecol])
            p.claim("vector", asum)
            for k in range(nk):
                stg = stages[k % 2]
                p.emit_dma("sync", lambda q, k=k, stg=stg, dram_ap=dram_ap: q.dma_start(
                    out=stg.ap, in_=dram_ap[:, k, :]), writes=[stg])
                p.emit("scalar", lambda e, stg=stg, q1=q1: e.activation(
                    q1.ap, stg.ap, AF.Abs), reads=[stg], writes=[q1])
                p.emit("vector", lambda e, k=k, q1=q1, asum=asum: e.tensor_reduce(
                    asum.ap[:, k:k + 1], q1.ap, mybir.AxisListType.X, ALU.add),
                    reads=[q1], writes=[asum] if k == nk - 1 else ())
            p.emit("tensor", lambda e, onecol=onecol, asum=asum, pq=pq, nk=nk: e.matmul(
                pq.ap[0:1, 0:nk], onecol.ap, asum.ap[:, 0:nk], start=True, stop=True),
                reads=[onecol, asum], writes=[pq])
            p.emit("vector", lambda e, pq=pq, nk=nk: e.tensor_reduce(
                rq["tot"].ap[:, 0:1], pq.ap[0:1, 0:nk], mybir.AxisListType.X, ALU.add),
                reads=[pq], writes=[rq["tot"]])
            p.emit("vector", lambda e, nk=nk, ncols=ncols: e.tensor_single_scalar(
                rq["m1"].ap[:, 0:1], rq["tot"].ap[:, 0:1], 1.0 / (nk * 128 * ncols),
                ALU.mult), reads=[rq["tot"]], writes=[rq["m1"]])
            p.emit("vector", lambda e: e.tensor_single_scalar(
                rq["m1"].ap[:, 0:1], rq["m1"].ap[:, 0:1], 1e-5, ALU.max),
                reads=[rq["m1"]], writes=[rq["m1"]])
            p.emit("vector", lambda e, mrow=mrow: e.tensor_copy(
                mrow.ap, rq["m1"].ap[:, 0:1]), reads=[rq["m1"]], writes=[mrow])
            p.emit("vector", lambda e: e.tensor_single_scalar(
                rq["thr"].ap[:, 0:1], rq["m1"].ap[:, 0:1], 0.5, ALU.mult),
                reads=[rq["m1"]], writes=[rq["thr"]])
            p.emit("tensor", lambda e, pq=pq: e.matmul(
                pq.ap[:, 0:1], onesrow.ap, rq["m1"].ap[:, 0:1], start=True, stop=True),
                reads=[onesrow, rq["m1"]], writes=[pq])
            p.emit("vector", lambda e, pq=pq, mcol=mcol: e.tensor_copy(
                mcol.ap, pq.ap[:, 0:1]), reads=[pq], writes=[mcol])
            p.emit("tensor", lambda e, pq=pq: e.matmul(
                pq.ap[:, 0:1], onesrow.ap, rq["thr"].ap[:, 0:1], start=True, stop=True),
                reads=[onesrow, rq["thr"]], writes=[pq])
            p.emit("vector", lambda e, pq=pq, thrcol=thrcol: e.tensor_copy(
                thrcol.ap[:, 0:1], pq.ap[:, 0:1]), reads=[pq], writes=[thrcol])
            p.emit("vector", lambda e, thrcol=thrcol: e.tensor_single_scalar(
                thrcol.ap[:, 1:2], thrcol.ap[:, 0:1], -1.0, ALU.mult),
                reads=[thrcol], writes=[thrcol])
            p.claim("vector", dst_bf)
            for k in range(nk):
                stg = stages[k % 2]
                p.emit_dma("sync", lambda q, k=k, stg=stg, dram_ap=dram_ap: q.dma_start(
                    out=stg.ap, in_=dram_ap[:, k, :]), writes=[stg])
                p.emit("vector", lambda e, stg=stg, q1=q1, thrcol=thrcol: e.tensor_scalar(
                    q1.ap, stg.ap, thrcol.ap[:, 0:1], None, ALU.is_gt),
                    reads=[stg, thrcol], writes=[q1])
                p.emit("vector", lambda e, stg=stg, q2=q2, thrcol=thrcol: e.tensor_scalar(
                    q2.ap, stg.ap, thrcol.ap[:, 1:2], None, ALU.is_lt),
                    reads=[stg, thrcol], writes=[q2])
                p.emit("vector", lambda e, q1=q1, q2=q2: e.tensor_tensor(
                    q1.ap, q1.ap, q2.ap, ALU.subtract),
                    reads=[q1, q2], writes=[q1])
                p.emit("vector", lambda e, k=k, q1=q1, dst_bf=dst_bf, ncols=ncols,
                       sc=g_col(k): e.tensor_scalar(
                           dst_bf.ap[:, k * ncols:(k + 1) * ncols], q1.ap, sc, None,
                           ALU.mult),
                       reads=[q1, pc], writes=[dst_bf] if k == nk - 1 else ())
            p.barrier()
        return mcol, mrow

    def phase_C1():
        NT = NREAL // WM
        with ExitStack() as stC:
            w1s = sb("w1s_bf", [128, 8 * HID], BF16, st=stC)
            m1col, m1row = quantize(ww1d, 8, HID, w1s, lambda k: col("fln1_g", k),
                                    stC, "w1")
            b1f_row = Buf(rowarena.ap[:, 0:HID], "b1f_row")
            psb = ps("psC1_sb", st=stC)
            pbb = ps("psC1_bb", st=stC)
            psbH = ps("psC1_sbH", st=stC)
            pbbH = ps("psC1_bbH", st=stC)
            pmC = [ps(f"psC1_m{i}", banks=2, st=stC) for i in range(2)]

            with ExitStack() as stC0:
                gi = sb("giC1", [128, 16], F32, st=stC0)
                fb1_r = load_row("fb1", "fb1", stC0)
                gib = sb("gibC1", [128, 8], BF16, st=stC0)
                brow = sb("browC1", [1, HID], F32, st=stC0)
                p.emit("vector", lambda e, gi=gi: e.reciprocal(
                    gi.ap[:, 0:8], col("fln1_g", 0, 8)), reads=[pc], writes=[gi])
                p.emit("vector", lambda e, gi=gi: e.tensor_tensor(
                    gi.ap[:, 8:16], col("fln1_b", 0, 8), gi.ap[:, 0:8], ALU.mult),
                    reads=[pc, gi], writes=[gi])
                p.emit("vector", lambda e, gi=gi, gib=gib: e.tensor_copy(
                    gib.ap, gi.ap[:, 8:16]), reads=[gi], writes=[gib])
                p.claim("vector", brow)
                for cc in range(8):
                    for k in range(8):
                        p.emit("tensor", lambda e, cc=cc, k=k, gib=gib: e.matmul(
                            psb.ap[0:1, 0:512], gib.ap[:, k:k + 1],
                            w1s.ap[:, k * HID + cc * 512: k * HID + cc * 512 + 512],
                            start=(k == 0), stop=(k == 7)),
                            reads=[gib, w1s], writes=[psb] if k == 0 else (),
                            accum=None if k == 0 else psb)
                    p.emit("vector", lambda e, cc=cc, brow=brow, fb1_r=fb1_r:
                           e.scalar_tensor_tensor(
                        brow.ap[:, cc * 512:(cc + 1) * 512], psb.ap[0:1, 0:512],
                        m1row.ap[0:1, 0:1],
                        fb1_r.ap[:, cc * 512:(cc + 1) * 512], ALU.mult, ALU.add),
                        reads=[psb, m1row, fb1_r], writes=[brow] if cc == 7 else ())
                p.emit("vector", lambda e, brow=brow: e.tensor_copy(b1f_row.ap, brow.ap),
                       reads=[brow], writes=[b1f_row])
                p.barrier()

            xmt = [sb(f"xmtC1_{i}", [128, 8 * WM], st=stC) for i in range(2)]
            twc = sb("twC1", [128, 8 * WM], st=stC)
            sqbfC = sb("sqbfC1", [128, 8 * WM], BF16, st=stC)
            zlnbf = [sb(f"zlnbfC1_{i}", [128, 8 * WM], BF16, st=stC) for i in range(2)]
            hC = sb("hC1", [128, 32 * WM], st=stC)
            hbfC = sb("hbfC1", [128, 32 * WM], BF16, st=stC)
            hsqC = sb("hsqC1", [128, 32 * WM], BF16, st=stC)
            hnst = sb("hnstC1", [128, 8 * WM], BF16, st=stC)

            def front_a(j):
                xcur = xmt[j % 2]
                rows = rows2[j % 2]
                zcur = zlnbf[j % 2]
                p.emit_dma("sync", lambda q, j=j, xcur=xcur: q.dma_start(
                    out=r3(xcur.ap), in_=xm_dram[:, :, j * WM:(j + 1) * WM]),
                    reads=[xm_d], writes=[xcur])
                ln_stats(rows, xcur, WM, 8, zcur, sqbfC, psb, composed=True)

            def front_b(j):
                xcur = xmt[j % 2]
                rows = rows2[j % 2]
                zcur = zlnbf[j % 2]
                bcast(rows, "negm", WM, pbb, 0, True)
                bcast(rows, "inv12", WM, pbb, WM, False)
                apply2(xcur, twc, WM, 8, pbb, 0, pbb, WM)
                p.emit("vector", lambda e, zcur=zcur: e.tensor_scalar(
                    zcur.ap, twc.ap, m1col.ap[:, 0:1], None, ALU.mult),
                    reads=[twc, m1col], writes=[zcur])

            def mm_group(j, op_):
                    zcur = zlnbf[j % 2]
                    bank = pmC[op_ % 2]
                    for half in range(2):
                        o = op_ * 2 + half
                        po = half * 512
                        p.emit("tensor", lambda e, o=o, po=po, bank=bank: e.matmul(
                            bank.ap[:, po:po + WM], b1f_row.ap[:, o * 128:(o + 1) * 128],
                            onesw_bf.ap[:, 0:WM], start=True, stop=False, skip_group_check=True),
                            reads=[b1f_row, onesw_bf],
                            writes=[bank] if half == 0 else (),
                            accum=None if half == 0 else bank)
                        for k in range(8):
                            p.emit("tensor", lambda e, o=o, k=k, po=po, bank=bank,
                                   zcur=zcur: e.matmul(
                                bank.ap[:, po:po + WM],
                                w1s.ap[:, k * HID + o * 128: k * HID + o * 128 + 128],
                                zcur.ap[:, k * WM:(k + 1) * WM],
                                start=False, stop=(k == 7), skip_group_check=True),
                                reads=[w1s, zcur], writes=(), accum=bank)
                    p.emit("scalar", lambda e, op_=op_, bank=bank: e.activation(
                        hC.ap[:, op_ * 2 * WM:(op_ + 1) * 2 * WM].rearrange(
                            "p (b w) -> p b w", b=2), pair_view(bank, WM), AF.Gelu),
                        reads=[bank], writes=[hC])

            def stats_block(hc):
                p.emit("vector", lambda e, hc=hc: e.tensor_copy(
                    hbfC.ap[:, hc * 8 * WM:(hc + 1) * 8 * WM],
                    hC.ap[:, hc * 8 * WM:(hc + 1) * 8 * WM]),
                    reads=[hC], writes=[hbfC])
                p.emit("scalar", lambda e, hc=hc: e.activation(
                    hsqC.ap[:, hc * 8 * WM:(hc + 1) * 8 * WM],
                    hC.ap[:, hc * 8 * WM:(hc + 1) * 8 * WM], AF.Square),
                    reads=[hC], writes=[hsqC])
                for h2 in range(8):
                    k = hc * 8 + h2
                    p.emit("tensor", lambda e, k=k: e.matmul(
                        psbH.ap[0:1, 0:WM], ones_bf.ap,
                        hbfC.ap[:, k * WM:(k + 1) * WM],
                        start=(k == 0), stop=(k == 31), skip_group_check=True),
                        reads=[ones_bf, hbfC],
                        writes=[psbH] if k == 0 else (),
                        accum=None if k == 0 else psbH)
                for h2 in range(8):
                    k = hc * 8 + h2
                    p.emit("tensor", lambda e, k=k: e.matmul(
                        psbH.ap[0:1, WM:2 * WM], ones_bf.ap,
                        hsqC.ap[:, k * WM:(k + 1) * WM],
                        start=False, stop=(k == 31), skip_group_check=True),
                        reads=[ones_bf, hsqC], writes=(), accum=psbH)

            def back_pre(j):
                p.claim("scalar", hC)
                for hc in range(0, 2):
                    for op_ in range(hc * 4, hc * 4 + 4):
                        mm_group(j, op_)
                    if hc >= 1:
                        stats_block(hc - 1)

            def back_post(j):
                rows = rows2[j % 2]
                for hc in range(2, 4):
                    for op_ in range(hc * 4, hc * 4 + 4):
                        mm_group(j, op_)
                    stats_block(hc - 1)
                stats_block(3)

                row_chain(rows, psbH, psbH, WM, WM, 4096, False)
                bcast(rows, "negm", WM, pbbH, 0, True)
                bcast(rows, "inv", WM, pbbH, WM, False)
                for hc in range(4):
                    p.emit("vector", lambda e, hc=hc: e.tensor_tensor(
                        r3(twc.ap), r3(hC.ap[:, hc * 8 * WM:(hc + 1) * 8 * WM]),
                        pbbH.ap[:, 0:WM].rearrange("p (o w) -> p o w", o=1).broadcast_to(
                            (128, 8, WM)), ALU.add),
                        reads=[hC, pbbH], writes=[twc])
                    p.emit("vector", lambda e: e.tensor_tensor(
                        r3(twc.ap), r3(twc.ap),
                        pbbH.ap[:, WM:2 * WM].rearrange("p (o w) -> p o w", o=1).broadcast_to(
                            (128, 8, WM)), ALU.mult),
                        reads=[twc, pbbH], writes=[twc])
                    p.emit("scalar", lambda e: e.activation(hnst.ap, twc.ap, AF.Copy),
                           reads=[twc], writes=[hnst])
                    p.emit_dma("gpsimd", lambda q, j=j, hc=hc: q.dma_start(
                        out=hn_dram[:, hc * 8:(hc + 1) * 8, j * WM:(j + 1) * WM],
                        in_=r3(hnst.ap)),
                        reads=[hnst], writes=[hn_d], group=True, lane_buf=hnst)

            def back1(j):
                back_pre(j)

            def back2(j):
                back_post(j)

            for j in range(NT):
                front_a(j)
                if j >= 1:
                    back1(j - 1)
                front_b(j)
                if j >= 1:
                    back2(j - 1)
            back1(NT - 1)
            back2(NT - 1)
            p.barrier()

    if "C1" in phases:
        phase_C1()

    def phase_C2():
        NT = NREAL // WM
        with ExitStack() as stC:
            w2s = sb("w2s_bf", [128, 32 * D], BF16, st=stC)
            m2col, m2row = quantize(ww2d, 32, D, w2s, lambda k: col("fln2_g", k),
                                    stC, "w2")
            c2eff_row = Buf(rowarena.ap[:, HID:HID + D], "c2eff_row")
            psb = ps("psC2_sb", st=stC)
            pbb = ps("psC2_bb", st=stC)
            pmD = [ps(f"psC2_m{i}", banks=2, st=stC) for i in range(2)]

            with ExitStack() as stC0:
                gi = sb("giC2", [128, 64], F32, st=stC0)
                fb2_r = load_row("fb2", "fb2", stC0)
                gib = sb("gibC2", [128, 32], BF16, st=stC0)
                m2inv = sb("m2inv", [1, 1], F32, st=stC0)
                brow = sb("browC2", [1, D], F32, st=stC0)
                p.emit("vector", lambda e, gi=gi: e.reciprocal(
                    gi.ap[:, 0:32], col("fln2_g", 0, 32)), reads=[pc], writes=[gi])
                p.emit("vector", lambda e, gi=gi: e.tensor_tensor(
                    gi.ap[:, 32:64], col("fln2_b", 0, 32), gi.ap[:, 0:32], ALU.mult),
                    reads=[pc, gi], writes=[gi])
                p.emit("vector", lambda e, gi=gi, gib=gib: e.tensor_copy(
                    gib.ap, gi.ap[:, 32:64]), reads=[gi], writes=[gib])
                p.claim("vector", brow)
                for cc in range(2):
                    for k in range(32):
                        p.emit("tensor", lambda e, cc=cc, k=k, gib=gib: e.matmul(
                            psb.ap[0:1, 0:512], gib.ap[:, k:k + 1],
                            w2s.ap[:, k * D + cc * 512: k * D + cc * 512 + 512],
                            start=(k == 0), stop=(k == 31)),
                            reads=[gib, w2s], writes=[psb] if k == 0 else (),
                            accum=None if k == 0 else psb)
                    p.emit("vector", lambda e, cc=cc, brow=brow: e.tensor_copy(
                        brow.ap[:, cc * 512:(cc + 1) * 512], psb.ap[0:1, 0:512]),
                        reads=[psb], writes=[brow] if cc == 1 else ())
                p.emit("vector", lambda e, m2inv=m2inv: e.reciprocal(
                    m2inv.ap, m2row.ap), reads=[m2row], writes=[m2inv])
                p.emit("vector", lambda e, brow=brow, m2inv=m2inv, fb2_r=fb2_r:
                       e.scalar_tensor_tensor(
                    brow.ap[:, 0:D], fb2_r.ap, m2inv.ap, brow.ap[:, 0:D],
                    ALU.mult, ALU.add),
                    reads=[fb2_r, m2inv, brow], writes=[brow])
                p.emit("vector", lambda e, brow=brow: e.tensor_copy(c2eff_row.ap,
                                                                    brow.ap[:, 0:D]),
                       reads=[brow], writes=[c2eff_row])
                p.barrier()

            hnt = [sb(f"hntC2_{i}", [128, 32 * WM], BF16, st=stC) for i in range(2)]
            xmt2 = [sb(f"xmtC2_{i}", [128, 8 * WM], st=stC) for i in range(3)]
            twd = sb("twC2", [128, 8 * WM], st=stC)
            xbfD = sb("xbfC2", [128, 8 * WM], BF16, st=stC)
            sqbfD = sb("sqbfC2", [128, 8 * WM], BF16, st=stC)
            yfC = [sb(f"yfC2_{i}", [128, 8 * WM], st=stC) for i in range(3)]

            def core(j):
                hcur = hnt[j % 2]
                xcur = xmt2[j % 3]
                ycur = yfC[j % 3]
                p.emit_dma("sync", lambda q, j=j, hcur=hcur: q.dma_start(
                    out=r3(hcur.ap, 32), in_=hn_dram[:, :, j * WM:(j + 1) * WM]),
                    reads=[hn_d], writes=[hcur])
                p.emit_dma("scalar", lambda q, j=j, xcur=xcur: q.dma_start(
                    out=r3(xcur.ap), in_=xm_dram[:, :, j * WM:(j + 1) * WM]),
                    reads=[xm_d], writes=[xcur])
                p.claim("vector", ycur)
                for op_ in range(4):
                    bank = pmD[op_ % 2]
                    for half in range(2):
                        o = op_ * 2 + half
                        po = half * 512
                        p.emit("tensor", lambda e, o=o, po=po, bank=bank: e.matmul(
                            bank.ap[:, po:po + WM], c2eff_row.ap[:, o * 128:(o + 1) * 128],
                            onesw_bf.ap[:, 0:WM], start=True, stop=False, skip_group_check=True),
                            reads=[c2eff_row, onesw_bf],
                            writes=[bank] if half == 0 else (),
                            accum=None if half == 0 else bank)
                        for k in range(32):
                            p.emit("tensor", lambda e, o=o, k=k, po=po, bank=bank,
                                   hcur=hcur: e.matmul(
                                bank.ap[:, po:po + WM],
                                w2s.ap[:, k * D + o * 128: k * D + o * 128 + 128],
                                hcur.ap[:, k * WM:(k + 1) * WM],
                                start=False, stop=(k == 31), skip_group_check=True),
                                reads=[w2s, hcur], writes=(), accum=bank)
                    p.emit("vector", lambda e, op_=op_, bank=bank, ycur=ycur: e.tensor_scalar(
                        r3(ycur.ap[:, op_ * 2 * WM:(op_ + 1) * 2 * WM], 2),
                        pair_view(bank, WM), m2col.ap[:, 0:1], None, ALU.mult),
                        reads=[bank, m2col], writes=[ycur] if op_ == 3 else ())

            def finish_a(j):
                ycur = yfC[j % 3]
                rows = rows2[j % 2]
                ln_stats(rows, ycur, WM, 8, xbfD, sqbfD, psb)

            def finish_b(j):
                xcur = xmt2[j % 3]
                ycur = yfC[j % 3]
                rows = rows2[j % 2]
                bcast(rows, "negm", WM, pbb, 0, True)
                bcast(rows, "inv", WM, pbb, WM, False)
                apply2(ycur, twd, WM, 8, pbb, 0, pbb, WM)
                p.emit("vector", lambda e, xcur=xcur, ycur=ycur: e.tensor_tensor(
                    ycur.ap, xcur.ap, twd.ap, ALU.add),
                    reads=[xcur, twd], writes=[ycur])
                p.emit_dma("gpsimd", lambda q, j=j, ycur=ycur: q.dma_start(
                    out=yd[:, :, j * WM:(j + 1) * WM], in_=r3(ycur.ap)),
                    reads=[ycur], writes=[y_out], group=True, lane_buf=ycur)

            for j in range(NT):
                core(j)
                if j >= 1:
                    finish_a(j - 1)
                if j >= 2:
                    finish_b(j - 2)
            finish_a(NT - 1)
            finish_b(NT - 2)
            finish_b(NT - 1)

    if "C2" in phases:
        phase_C2()

    for lane, v in p.lane_val.items():
        if v > 0:
            p.wait_ref("gpsimd", ("lane", lane, v))

    with nc.Block() as block:
        p.finalize(block)
    stack.close()
    return nc, p



def _pack_pcols(ins):
    cols = np.zeros((128, PARAM_COLS), np.float32)

    def put(nm, arr):
        o = PARAM_OFF[nm]
        c = _cols(arr)
        cols[:, o:o + c.shape[1]] = c

    put("pw_b", ins["L0_attn_pw_b"])
    put("dw_b", ins["L0_attn_dw_b"])
    put("ln1_g", ins["L0_attn_ln1_g"])
    put("ln1_b", ins["L0_attn_ln1_b"])
    put("ln2_g", ins["L0_attn_ln2_g"])
    put("ln2_b", ins["L0_attn_ln2_b"])
    put("ln3_g", ins["L0_attn_ln3_g"])
    put("ln3_b", ins["L0_attn_ln3_b"])
    put("mlp_b2", ins["L0_attn_mlp_b2"])
    put("fb2", ins["L0_ffn_b2"])
    put("fln1_g", ins["L0_ffn_ln1_g"])
    put("fln1_b", ins["L0_ffn_ln1_b"])
    put("f_b", ins["L0_attn_f_b"])
    put("mlp_b1", ins["L0_attn_mlp_b1"])
    put("fb1", ins["L0_ffn_b1"])
    put("fln2_g", ins["L0_ffn_ln2_g"])
    put("fln2_b", ins["L0_ffn_ln2_b"])
    dw = np.asarray(ins["L0_attn_dw_w"], np.float32)[:, 0, :]
    dwt = dw.reshape(8, 128, 4).transpose(1, 0, 2).reshape(128, 32)
    cols[:, PARAM_OFF["dwtaps"]:PARAM_OFF["dwtaps"] + 32] = dwt
    return np.ascontiguousarray(cols)


def _pack_prows(ins):
    rows = np.zeros((1, PROW_COLS), np.float32)

    def put(nm, arr):
        o = PROW_OFF[nm]
        a = np.asarray(arr, np.float32).reshape(-1)
        rows[0, o:o + a.size] = a

    put("f_b", ins["L0_attn_f_b"])
    put("pw_b", ins["L0_attn_pw_b"])
    put("mlp_b1", ins["L0_attn_mlp_b1"])
    put("mlp_b2", ins["L0_attn_mlp_b2"])
    put("fb1", ins["L0_ffn_b1"])
    put("fb2", ins["L0_ffn_b2"])
    put("dw_b", ins["L0_attn_dw_b"])
    put("ln1_b", ins["L0_attn_ln1_b"])
    put("dwtaps", np.asarray(ins["L0_attn_dw_w"], np.float32)[:, 0, :].reshape(-1))
    return np.ascontiguousarray(rows)


def kernel(**ins):
    x = np.asarray(ins["L0_attn_x"], dtype=np.float32)

    shared = {
        "pcols": _pack_pcols(ins),
        "prows": _pack_prows(ins),
        "ident": np.eye(128, dtype=np.float32),
        "onesrow": np.ones((1, 128), np.float32),
        "wpw": _tiles(np.asarray(ins["L0_attn_pw_w"]).T),
        "wfw": _tiles(np.asarray(ins["L0_attn_f_w"]).T),
        "wdp": _tiles(np.asarray(ins["L0_attn_dp_w"]).T),
        "wm1": _tiles(np.asarray(ins["L0_attn_mlp_w1"]).T),
        "wm2": _tiles(np.asarray(ins["L0_attn_mlp_w2"]).T),
        "ww1": _tiles(np.asarray(ins["L0_ffn_w1"]).T),
        "ww2": _tiles(np.asarray(ins["L0_ffn_w2"]).T),
    }

    def shard_x(core):
        b, half = core // 2, core % 2
        if half == 0:
            s = np.concatenate([np.zeros((WARM, D), np.float32), x[b, 0:NREAL]], 0)
        else:
            s = x[b, NREAL - WARM: 2 * NREAL]
        return np.ascontiguousarray(s.T.reshape(8, 128, T).transpose(1, 0, 2))

    def shard_mask(core):
        half = core % 2
        m = np.zeros((128, 2), np.float32)
        if half == 0:
            m[:, 1] = 1.0
        else:
            m[:, 0] = 1.0
        return m

    in_maps = []
    for core in range(8):
        d = dict(shared)
        d["x"] = shard_x(core)
        d["smask"] = shard_mask(core)
        in_maps.append(d)

    nc, _ = build_program()
    res = run_bass_kernel_spmd(nc, in_maps, core_ids=list(range(8)))

    out = np.empty((4, 2 * NREAL, D), np.float32)
    for core in range(8):
        b, half = core // 2, core % 2
        yk = res.results[core]["y"]
        out[b, half * NREAL:(half + 1) * NREAL] = yk.transpose(2, 1, 0).reshape(NREAL, D)
    return out


if __name__ == "__main__":
    nc_, p_ = build_program()
    import json
    print(json.dumps(p_.stats(), indent=1))
